# version 15
# baseline (speedup 1.0000x reference)
import numpy as np
from contextlib import ExitStack

import concourse.bass as bass
import concourse.mybir as mybir
from concourse.bass_utils import run_bass_kernel_spmd

F32 = mybir.dt.float32
BF16 = mybir.dt.bfloat16
AF = mybir.ActivationFunctionType
ALU = mybir.AluOpType
AX = mybir.AxisListType

NCORES = 8
TC = 2048
NGRP = 2
TG = TC // NGRP
NTT = TC // 128
TT_G = TG // 128
D = 2048
KT = D // 128
I = 7168
NIT = I // 128
E = 8
R = 64
NST = 256
LEV = 8
EPS = 1e-5
SCALING = 16.0 / 64.0

WC_BG = 0
WC_BU = WC_BG + KT * 128
WC_GB = WC_BU + KT * 128
WC_UB = WC_GB + E * 128
WC_DA = WC_UB + E * 128
WCOLS = WC_DA + E * 64

ENGS = ("gpsimd", "scalar", "vector", "sync", "tensor")
SA_LIMIT = 99
DSEMS = ("dhw0", "dhw1", "dwb0", "dwb1", "dwb2", "dhg0", "dhg1", "dx0", "dx1a", "dx1b", "dhx0",
         "dhx1", "dxb0", "dxb1", "dxb2", "dxb3", "dya", "dyb", "dini", "dtapa", "dtapb")


class Trk:

    def __init__(self, nc, sems):
        self.nc = nc
        self.sem = sems

    def reset(self, current, engobj):
        self.current = current
        self.engobj = engobj
        self.cnt = {e: 0 for e in ENGS}
        for s in DSEMS:
            self.cnt[s] = 0
        self.seq = {e: 0 for e in ENGS}
        self.incseqs = {e: [] for e in ENGS}
        self.w = {}
        self.r = {}
        self.waited = {}

    def _wait(self, semname, count):
        if count <= self.waited.get(semname, 0):
            return
        self.waited[semname] = count
        self.engobj.wait_ge(self.sem[semname], count)

    def _resolve(self, k, tok):
        semname, count, seq = tok
        if count is not None:
            return semname, count
        import bisect
        lst = self.incseqs[semname]
        idx = bisect.bisect_left(lst, seq)
        if idx >= len(lst):
            raise AssertionError(
                f"dep on un-inc'd {semname} op (key={k}) with no later inc yet")
        return semname, idx + 1

    def op(self, eng, fn, reads=(), writes=(), inc=True):
        deps = []
        for k in reads:
            if k in self.w:
                deps.append((k, self.w[k]))
        for k in writes:
            if k in self.w:
                deps.append((k, self.w[k]))
            for tok in self.r.get(k, {}).values():
                deps.append((k, tok))
        need = {}
        for k, tok in deps:
            if tok[0] == eng == "tensor":
                continue
            semname, count = self._resolve(k, tok)
            need[semname] = max(need.get(semname, 0), count)
        dma = isinstance(inc, str)
        if eng == self.current:
            for semname in sorted(need):
                self._wait(semname, need[semname])
            ins = fn()
        else:
            ins = None
        myseq = self.seq[eng]
        self.seq[eng] += 1
        if dma:
            semname = inc
            self.cnt[semname] += 16
            if ins is not None:
                ins.then_inc(self.sem[semname], 16)
            tok = (semname, self.cnt[semname], myseq)
        elif inc:
            self.cnt[eng] += 1
            self.incseqs[eng].append(myseq)
            if ins is not None:
                ins.then_inc(self.sem[eng], 1)
            tok = (eng, self.cnt[eng], myseq)
        else:
            tok = (eng, None, myseq)
        for k in writes:
            self.w[k] = tok
            self.r[k] = {}
        for k in reads:
            self.r.setdefault(k, {})[tok[0]] = tok
        return ins

    def barrier(self):
        for semname in list(ENGS) + list(DSEMS):
            c = self.cnt[semname]
            if c > 0 and semname != self.current:
                self._wait(semname, c)

    def mark_done(self, keys, semname):
        tok = (semname, self.cnt[semname], 0)
        for k in keys:
            if k in self.w:
                self.w[k] = tok


def psk(c0, c1):
    return [f"B{b}" for b in range(c0 // 512, (c1 + 511) // 512)]


def build_core_program(taps=(), stop_after=None):
    nc = bass.Bass("TRN2", target_bir_lowering=False, debug=False)
    taps = set(taps)

    u_h = nc.dram_tensor("u", [TC, D], F32, kind="ExternalInput").ap()
    wslab_h = nc.dram_tensor("wslab", [NIT, 128, WCOLS], F32, kind="ExternalInput").ap()
    xau_h = nc.dram_tensor("xau", [E, 128, KT * 128], F32, kind="ExternalInput").ap()
    bd_h = nc.dram_tensor("bdslab", [4, NIT, 128, 512], F32, kind="ExternalInput").ap()
    dbcat_h = nc.dram_tensor("dbcat", [4, 128, D], F32, kind="ExternalInput").ap()
    wbr_h = nc.dram_tensor("wbr", [D, R], F32, kind="ExternalInput").ap()
    wbs_h = nc.dram_tensor("wbs", [R, NST], F32, kind="ExternalInput").ap()
    wcr_h = nc.dram_tensor("wcr", [NST, R], F32, kind="ExternalInput").ap()
    wcm_h = nc.dram_tensor("wcm", [R, D], F32, kind="ExternalInput").ap()
    afac_h = nc.dram_tensor("afac", [1, 32], F32, kind="ExternalInput").ap()
    dvec_h = nc.dram_tensor("dvecr", [128, KT], F32, kind="ExternalInput").ap()
    cwt_h = nc.dram_tensor("confwt", [D, E], F32, kind="ExternalInput").ap()
    cb_h = nc.dram_tensor("confb", [1, E], F32, kind="ExternalInput").ap()
    wl_h = nc.dram_tensor("wealth", [1, E], F32, kind="ExternalInput").ap()
    y_h = nc.dram_tensor("y", [TC, D], F32, kind="ExternalOutput").ap()
    x1s_h = nc.dram_tensor("x1scr", [NTT, 128, D], F32, kind="Internal").ap()
    hx_h = nc.dram_tensor("hmixscr", [NIT, 128, TC], BF16, kind="Internal").ap()
    haw_h = nc.dram_tensor("hawscr", [128, 4, NGRP, TG], BF16, kind="Internal").ap()
    tap_h = {}
    if "x1" in taps:
        tap_h["x1"] = nc.dram_tensor("tap_x1", [TC, D], F32, kind="ExternalOutput").ap()
    if "weall" in taps:
        tap_h["weall"] = nc.dram_tensor("tap_weall", [TC, E], F32, kind="ExternalOutput").ap()
    if "hmix" in taps:
        tap_h["hmix"] = nc.dram_tensor("tap_hmix", [NIT, 128, TC], BF16, kind="ExternalOutput").ap()
    if "lnt" in taps:
        tap_h["lnt"] = nc.dram_tensor("tap_lnt", [NTT, 128, D], F32, kind="ExternalOutput").ap()

    es = ExitStack()
    sb = lambda n, shape, dt: es.enter_context(nc.sbuf_tensor(n, shape, dt))
    with es:
        sems = {}
        for e_ in ENGS:
            sems[e_] = es.enter_context(nc.semaphore(f"sem_{e_}"))
        for s_ in DSEMS:
            sems[s_] = es.enter_context(nc.semaphore(f"semd_{s_}"))
        tk = Trk(nc, sems)

        wbuf = [sb(f"wbuf{i}", [128, WCOLS], BF16) for i in range(3)]
        wbr = sb("wbr_s", [128, KT * R], F32)
        wcm = sb("wcm_s", [R, D], F32)
        wcr = sb("wcr_s", [128, 2 * R], F32)
        wbsA = sb("wbsA", [R, NST], F32)
        wbsB = sb("wbsB", [R, NST], F32)
        wbsT = sb("wbsT", [128, 2 * R], F32)
        wmid = sb("wmid", [R, R], F32)
        afb = sb("afb", [R, 32], F32)
        cwt = sb("cwt_s", [128, KT * E], F32)
        cbb = sb("cbb", [128, E], F32)
        wlb = sb("wlb", [128, E], F32)
        dvr = sb("dvr", [128, KT], F32)
        ident_f = sb("ident_f", [128, 128], F32)
        ones1 = sb("ones1", [1, 128], F32)
        ones8 = sb("ones8", [128, E], F32)
        ohc = sb("ohc", [E, E * 128], F32)
        epsb = sb("epsb", [128, 1], F32)
        ph1 = ExitStack()
        sb1 = lambda n, shape, dt: ph1.enter_context(nc.sbuf_tensor(n, shape, dt))
        ln1t = sb1("ln1t", [128, KT * 1024], BF16)
        xga = sb1("xga", [128, 16 * 1024], BF16)
        webc = sb1("webc", [128, E * 1024], BF16)
        hab = sb1("hab", [128, 4 * 1024], F32)
        saes = ExitStack()
        sbA = lambda n, shape, dt: saes.enter_context(nc.sbuf_tensor(n, shape, dt))
        x0t = [sbA(f"x0t{i}", [128, D], F32) for i in range(2)]
        lnt = [sbA(f"lnt{i}", [128, D], F32) for i in range(2)]
        sa = []
        for i in range(2):
            sa.append({nm: sbA(f"{nm}{i}", [128, E], F32)
                       for nm in ("t8", "pad1", "pad2", "rep", "eqA", "eqB", "wal", "bidt")})
            for nm in ("d12", "r1", "r2", "mstat", "vstat", "sstat", "rstd"):
                sa[i][nm] = sbA(f"{nm}{i}", [128, 1], F32)
            sa[i]["wet"] = sbA(f"wet{i}", [E, 128], F32)
            sa[i]["s1b"] = sbA(f"s1b{i}", [R, 128], F32)
            sa[i]["s2b"] = sbA(f"s2b{i}", [R, 128], F32)
            sa[i]["c8b"] = sbA(f"c8b{i}", [E, 128], F32)
        saes.close()
        hmixw = [sb1(f"hmixw{i}", [128, 1024], BF16) for i in range(2)]
        sgb = [sb1(f"sgb{i}", [128, 1024], BF16) for i in range(2)]
        ubb = [sb1(f"ubb{i}", [128, 1024], BF16) for i in range(2)]
        h8 = sb1("h8", [128, E * 1024], BF16)
        dabuf = [sb1(f"dabuf{i}", [128, E * R], BF16) for i in range(2)]
        ph1.close()
        outacc = sb("outacc", [128, 4 * 2048], F32)
        hawb = sb("hawb2", [128, 4 * TC], BF16)
        hmg = [sb(f"hmg{i}", [128, 7 * 2048], BF16) for i in range(2)]
        x1buf = [sb(f"x1buf{i}", [128, 512], F32) for i in range(4)]
        outrow = [sb(f"outrow{i}", [128, 512], F32) for i in range(4)]

        PS = es.enter_context(nc.psum_tensor([128, 4096], F32))

        blk = es.enter_context(nc.Block())
        EN = {}

        def op(eng, fn, reads=(), writes=(), inc=True):
            return tk.op(eng, fn, reads=reads, writes=writes, inc=inc)

        def emit_init():
            op("gpsimd", lambda: EN["gpsimd"].iota(
                sa[0]["mstat"][:], [[1, 1]], base=0, channel_multiplier=1,
                allow_small_or_imprecise_dtypes=True), writes=["mstat0"])
            op("gpsimd", lambda: EN["gpsimd"].iota(
                lnt[0][:, 0:128], [[1, 128]], base=0, channel_multiplier=0,
                allow_small_or_imprecise_dtypes=True), writes=["lnt0_sl0"])
            op("vector", lambda: EN["vector"].tensor_single_scalar(
                ident_f[:], lnt[0][:, 0:128], sa[0]["mstat"][:, 0:1], ALU.is_equal),
                reads=["lnt0_sl0", "mstat0"], writes=["ident_f"])
            op("vector", lambda: EN["vector"].memset(ones1[:], 1.0), writes=["ones1"])
            op("vector", lambda: EN["vector"].memset(epsb[:], EPS), writes=["epsb"])
            op("vector", lambda: EN["vector"].memset(ones8[:], 1.0), writes=["ones8"])
            op("gpsimd", lambda: EN["gpsimd"].iota(
                ohc[:].rearrange("p (b c) -> p b c", b=E), [[1, E], [0, 128]],
                base=0, channel_multiplier=0,
                allow_small_or_imprecise_dtypes=True), writes=["ohc"])
            op("vector", lambda: EN["vector"].tensor_single_scalar(
                ohc[:], ohc[:], sa[0]["mstat"][0:E, 0:1], ALU.is_equal),
                reads=["ohc", "mstat0"], writes=["ohc"])
            op("sync", lambda: EN["sync"].dma_start(
                wbr[:].rearrange("p (k r) -> p k r", r=R),
                wbr_h.rearrange("(k p) r -> p k r", p=128)), writes=["wbr"], inc="dini")
            op("sync", lambda: EN["sync"].dma_start(wcm[:], wcm_h[:, :]), writes=["wcm"], inc="dini")
            op("sync", lambda: EN["sync"].dma_start(
                wcr[:].rearrange("p (k r) -> p k r", r=R),
                wcr_h.rearrange("(k p) r -> p k r", p=128)), writes=["wcr"], inc="dini")
            op("sync", lambda: EN["sync"].dma_start(wbsA[:], wbs_h[:, :]), writes=["wbsA"], inc="dini")
            op("sync", lambda: EN["sync"].dma_start(
                cwt[:].rearrange("p (k e) -> p k e", e=E),
                cwt_h.rearrange("(k p) e -> p k e", p=128)), writes=["cwt"], inc="dini")
            op("sync", lambda: EN["sync"].dma_start(afb[0:1, :], afac_h[:, :]), writes=["afb"], inc="dini")
            op("sync", lambda: EN["sync"].dma_start(cbb[0:1, :], cb_h[:, :]), writes=["cbb"], inc="dini")
            op("sync", lambda: EN["sync"].dma_start(wlb[0:1, :], wl_h[:, :]), writes=["wlb"], inc="dini")
            op("sync", lambda: EN["sync"].dma_start(dvr[:], dvec_h[:, :]), writes=["dvr"], inc="dini")
            tk.mark_done(["wbr", "wcm", "wcr", "wbsA", "cwt", "afb", "cbb", "wlb", "dvr"], "dini")
            op("tensor", lambda: EN["tensor"].matmul(
                PS[:, 3072:3104], ones1[:], afb[0:1, :], start=True, stop=True),
                reads=["ones1", "afb"], writes=psk(3072, 3104), inc=True)
            op("tensor", lambda: EN["tensor"].matmul(
                PS[:, 3104:3112], ones1[:], cbb[0:1, :], start=True, stop=True),
                reads=["ones1", "cbb"], writes=psk(3104, 3112), inc=True)
            op("tensor", lambda: EN["tensor"].matmul(
                PS[:, 3112:3120], ones1[:], wlb[0:1, :], start=True, stop=True),
                reads=["ones1", "wlb"], writes=psk(3112, 3120), inc=True)
            op("vector", lambda: EN["vector"].tensor_copy(afb[:, :], PS[0:R, 3072:3104]),
               reads=psk(3072, 3104), writes=["afb"])
            op("vector", lambda: EN["vector"].tensor_copy(cbb[:, :], PS[:, 3104:3112]),
               reads=psk(3104, 3112), writes=["cbb"])
            op("vector", lambda: EN["vector"].tensor_copy(wlb[:, :], PS[:, 3112:3120]),
               reads=psk(3112, 3120), writes=["wlb"])
            src, dst, srck, dstk = wbsA, wbsB, "wbsA", "wbsB"
            for lvl in range(LEV):
                w = 1 << (LEV - 1 - lvl)
                for j in range(2):
                    a0 = afb[0:R, (lvl * 4 + j):(lvl * 4 + j) + 1]
                    a1 = afb[0:R, (lvl * 4 + 2 + j):(lvl * 4 + 2 + j) + 1]
                    vsrc = src[:, 0:NST].rearrange("p (h two w) -> p h two w", two=2, w=w)
                    vdst = dst[:, 0:NST].rearrange("p (h two w) -> p h two w", two=2, w=w)
                    scr = lnt[0][0:R, 0:NST // 2].rearrange("p (h w) -> p h w", w=w)
                    op("vector", lambda vsrc=vsrc, a0=a0, scr=scr: EN["vector"].tensor_single_scalar(
                        scr, vsrc[:, :, 0, :], a0, ALU.mult),
                        reads=[srck, "afb"], writes=["lnt0_scr"] + [f"lnt0_sl{k}" for k in range(2)])
                    op("vector", lambda vsrc=vsrc, vdst=vdst, a1=a1, j=j, scr=scr:
                        EN["vector"].scalar_tensor_tensor(
                            vdst[:, :, j, :], vsrc[:, :, 1, :], a1, scr, ALU.mult, ALU.add),
                        reads=[srck, "afb", "lnt0_scr"], writes=[dstk])
                src, dst, srck, dstk = dst, src, dstk, srck
            for k in range(2):
                op("tensor", lambda k=k, src=src: EN["tensor"].transpose(
                    PS[:, 3072 + k * 512: 3072 + k * 512 + 64],
                    src[0:R, k * 128:(k + 1) * 128], ident_f[0:R, 0:R]),
                    reads=[srck, "ident_f"], writes=psk(3072 + k * 512, 3072 + k * 512 + 64),
                    inc=True)
            op("vector", lambda: EN["vector"].tensor_copy(wbsT[:, 0:R], PS[:, 3072:3136]),
               reads=psk(3072, 3136), writes=["wbsT"])
            op("vector", lambda: EN["vector"].tensor_copy(wbsT[:, R:2 * R], PS[:, 3584:3648]),
               reads=psk(3584, 3648), writes=["wbsT"])
            for k in range(2):
                op("tensor", lambda k=k: EN["tensor"].matmul(
                    PS[0:R, 3072:3136], wbsT[:, k * R:(k + 1) * R],
                    wcr[:, k * R:(k + 1) * R], start=(k == 0), stop=(k == 1)),
                    reads=["wbsT", "wcr"], writes=psk(3072, 3136), inc=(k == 1))
            op("vector", lambda: EN["vector"].tensor_copy(wmid[:, :], PS[0:R, 3072:3136]),
               reads=psk(3072, 3136), writes=["wmid"])

        def sa_layernorm(P, x0, ln, LNSL):
            S = sa[P]
            op("vector", lambda: EN["vector"].reduce_sum(S["sstat"][:], x0[:], axis=AX.X),
               reads=[f"x0t{P}"], writes=[f"sstat{P}"])
            op("scalar", lambda: EN["scalar"].activation(
                ln[:], x0[:], AF.Square, accum_out=S["vstat"][:]),
                reads=[f"x0t{P}"], writes=[f"vstat{P}", f"lnt{P}_scr"] + LNSL)
            op("vector", lambda: EN["vector"].tensor_scalar_mul(S["mstat"][:], S["sstat"][:], 1.0 / D),
               reads=[f"sstat{P}"], writes=[f"mstat{P}"])
            op("vector", lambda: EN["vector"].tensor_scalar_mul(S["vstat"][:], S["vstat"][:], 1.0 / D),
               reads=[f"vstat{P}"], writes=[f"vstat{P}"])
            op("vector", lambda: EN["vector"].scalar_tensor_tensor(
                S["rstd"][:], S["mstat"][:], 1.0, S["mstat"][:], ALU.mult, ALU.mult),
                reads=[f"mstat{P}"], writes=[f"rstd{P}"])
            op("vector", lambda: EN["vector"].tensor_sub(S["vstat"][:], S["vstat"][:], S["rstd"][:]),
               reads=[f"vstat{P}", f"rstd{P}"], writes=[f"vstat{P}"])
            op("scalar", lambda: EN["scalar"].activation(S["rstd"][:], S["vstat"][:], AF.Sqrt,
                                                         bias=epsb[:, 0:1]),
               reads=[f"vstat{P}", "epsb"], writes=[f"rstd{P}"])
            op("vector", lambda: EN["vector"].reciprocal(S["rstd"][:], S["rstd"][:]),
               reads=[f"rstd{P}"], writes=[f"rstd{P}"])
            op("vector", lambda: EN["vector"].tensor_scalar(
                ln[:], x0[:], S["mstat"][:, 0:1], S["rstd"][:, 0:1], ALU.subtract, ALU.mult),
                reads=[f"x0t{P}", f"mstat{P}", f"rstd{P}"], writes=[f"lnt{P}_scr"] + LNSL)

        def sa_transpose_ln(P, UB, ln, ttl, to_bf):
            for q in range(KT // 4):
                qb = UB + (q % 2) * 512
                kts = list(range(4 * q, 4 * q + 4))
                for j, kt in enumerate(kts):
                    op("tensor", lambda kt=kt, slot=qb + j * 128, ln=ln: EN["tensor"].transpose(
                        PS[:, slot:slot + 128], ln[:, kt * 128:(kt + 1) * 128], ident_f[:]),
                        reads=[f"lnt{P}_sl{kt}", "ident_f"], writes=psk(qb, qb + 512),
                        inc=(j == 3))
                op("vector", lambda q=q, qb=qb, ln=ln: EN["vector"].tensor_copy(
                    ln[:, q * 512:(q + 1) * 512], PS[:, qb:qb + 512]),
                    reads=psk(qb, qb + 512),
                    writes=[f"lnt{P}_sl{kt}" for kt in kts])
                if to_bf:
                    op("scalar", lambda q=q, ln=ln, ttl=ttl: EN["scalar"].copy(
                        ln1t[:].rearrange("p (k c) -> p k c", c=1024)[:, 4 * q:4 * q + 4, ttl * 128:(ttl + 1) * 128],
                        ln[:, q * 512:(q + 1) * 512].rearrange("p (k c) -> p k c", c=128)),
                        reads=[f"lnt{P}_sl{kt}" for kt in kts],
                        writes=[f"ln1t_{kt}" for kt in kts])

        def sa_seg0(g, tile):
            P = tile % 2
            op("sync", lambda tile=tile, P=P: EN["sync"].dma_start(
                x0t[P][:], u_h[tile * 128:(tile + 1) * 128, :]),
                writes=[f"x0t{P}"], inc="dx0")
            sa_layernorm(P, x0t[P], lnt[P], [f"lnt{P}_sl{k}" for k in range(KT)])

        def sa_seg1(g, tile):
            P = tile % 2
            UB = 3072 if P == 0 else 0
            S = sa[P]
            sa_transpose_ln(P, UB, lnt[P], tile % TT_G, False)
            for kt in range(KT):
                op("tensor", lambda kt=kt, P=P, UB=UB: EN["tensor"].matmul(
                    PS[0:R, UB + 512:UB + 640], wbr[:, kt * R:(kt + 1) * R],
                    lnt[P][:, kt * 128:(kt + 1) * 128],
                    start=(kt == 0), stop=(kt == KT - 1)),
                    reads=[f"lnt{P}_sl{kt}", "wbr"], writes=psk(UB + 512, UB + 640),
                    inc=(kt == KT - 1))
            op("vector", lambda S=S, UB=UB: EN["vector"].tensor_copy(
                S["s1b"][:, :], PS[0:R, UB + 512:UB + 640]),
                reads=psk(UB + 512, UB + 640), writes=[f"s1b{P}"])
            op("tensor", lambda S=S, UB=UB: EN["tensor"].matmul(
                PS[0:R, UB + 512:UB + 640], wmid[:, :], S["s1b"][:, :], start=True, stop=True),
                reads=["wmid", f"s1b{P}"], writes=psk(UB + 512, UB + 640), inc=True)
            op("vector", lambda S=S, UB=UB: EN["vector"].tensor_copy(
                S["s2b"][:, :], PS[0:R, UB + 512:UB + 640]),
                reads=psk(UB + 512, UB + 640), writes=[f"s2b{P}"])

        def sa_seg2(g, tile):
            P = tile % 2
            UB = 3072 if P == 0 else 0
            S = sa[P]
            for kt in range(KT):
                slot = UB + (kt % 2) * 512
                op("tensor", lambda kt=kt, slot=slot, S=S: EN["tensor"].matmul(
                    PS[:, slot:slot + 128], wcm[:, kt * 128:(kt + 1) * 128], S["s2b"][:, :],
                    start=True, stop=True),
                    reads=["wcm", f"s2b{P}"], writes=psk(slot, slot + 128), inc=True)
                op("vector", lambda kt=kt, slot=slot, P=P: EN["vector"].scalar_tensor_tensor(
                    lnt[P][:, kt * 128:(kt + 1) * 128], lnt[P][:, kt * 128:(kt + 1) * 128],
                    dvr[:, kt:kt + 1], PS[:, slot:slot + 128], ALU.mult, ALU.add),
                    reads=[f"lnt{P}_sl{kt}", "dvr"] + psk(slot, slot + 128),
                    writes=[f"lnt{P}_sl{kt}"])
            for half in range(2):
                for k8 in range(8):
                    kt = half * 8 + k8
                    slot = UB + k8 * 128
                    op("tensor", lambda kt=kt, slot=slot, P=P: EN["tensor"].transpose(
                        PS[:, slot:slot + 128], lnt[P][:, kt * 128:(kt + 1) * 128], ident_f[:]),
                        reads=[f"lnt{P}_sl{kt}", "ident_f"], writes=psk(slot, slot + 128),
                        inc=(k8 % 4 == 3))
                op("vector", lambda half=half, P=P, UB=UB: EN["vector"].tensor_add(
                    x0t[P][:, half * 1024:(half + 1) * 1024],
                    x0t[P][:, half * 1024:(half + 1) * 1024], PS[:, UB:UB + 1024]),
                    reads=[f"x0t{P}"] + psk(UB, UB + 1024), writes=[f"x0t{P}"])
            op("sync", lambda tile=tile, P=P: EN["sync"].dma_start(
                x1s_h[tile, :, :], x0t[P][:]),
                reads=[f"x0t{P}"], writes=[f"hbm_x1_{tile}"],
                inc="dx1a" if tile % 2 == 0 else "dx1b")
            if "x1" in taps:
                op("sync", lambda tile=tile, P=P: EN["sync"].dma_start(
                    tap_h["x1"][tile * 128:(tile + 1) * 128, :], x0t[P][:]),
                    reads=[f"x0t{P}"], writes=["ser_ta" if tile % 2 == 0 else "ser_tb"],
                    inc="dtapa" if tile % 2 == 0 else "dtapb")

        def sa_seg3(g, tile):
            P = tile % 2
            UB = 3072 if P == 0 else 0
            sa_layernorm(P, x0t[P], lnt[P], [f"lnt{P}_sl{k}" for k in range(KT)])
            sa_transpose_ln(P, UB, lnt[P], tile % TT_G, True)

        def sa_seg4(g, tile):
            P = tile % 2
            UB = 3072 if P == 0 else 0
            S = sa[P]
            ttl = tile % TT_G
            for kt in range(KT):
                op("tensor", lambda kt=kt, P=P, UB=UB: EN["tensor"].matmul(
                    PS[0:E, UB + 512:UB + 640], cwt[:, kt * E:(kt + 1) * E],
                    lnt[P][:, kt * 128:(kt + 1) * 128],
                    start=(kt == 0), stop=(kt == KT - 1)),
                    reads=[f"lnt{P}_sl{kt}", "cwt"], writes=psk(UB + 512, UB + 640),
                    inc=(kt == KT - 1))
            op("vector", lambda S=S, UB=UB: EN["vector"].tensor_copy(
                S["c8b"][:, :], PS[0:E, UB + 512:UB + 640]),
                reads=psk(UB + 512, UB + 640), writes=[f"c8b{P}"])
            op("tensor", lambda S=S, UB=UB: EN["tensor"].transpose(
                PS[:, UB + 512:UB + 520], S["c8b"][:, :], ident_f[0:E, 0:E]),
                reads=[f"c8b{P}", "ident_f"], writes=psk(UB + 512, UB + 520), inc=True)
            op("vector", lambda S=S, UB=UB: EN["vector"].tensor_copy(
                S["bidt"][:], PS[:, UB + 512:UB + 520]),
                reads=psk(UB + 512, UB + 520), writes=[f"bidt{P}"])
            op("vector", lambda S=S: EN["vector"].tensor_add(S["bidt"][:], S["bidt"][:], cbb[:, :]),
               reads=[f"bidt{P}", "cbb"], writes=[f"bidt{P}"])
            op("scalar", lambda S=S: EN["scalar"].activation(S["bidt"][:], S["bidt"][:], AF.Sigmoid),
               reads=[f"bidt{P}"], writes=[f"bidt{P}"])
            op("vector", lambda S=S: EN["vector"].tensor_mul(S["bidt"][:], S["bidt"][:], wlb[:, :]),
               reads=[f"bidt{P}", "wlb"], writes=[f"bidt{P}"])
            op("vector", lambda S=S: EN["vector"].max(S["t8"][:], S["bidt"][:]),
               reads=[f"bidt{P}"], writes=[f"t8{P}"])
            op("vector", lambda S=S: EN["vector"].tensor_single_scalar(
                S["pad1"][:], ones8[:], S["t8"][:, 0:1], ALU.mult),
                reads=["ones8", f"t8{P}"], writes=[f"pad1{P}"])
            op("vector", lambda S=S: EN["vector"].match_replace(
                S["rep"][:], S["pad1"][:, :], S["bidt"][:], -1e30),
                reads=[f"pad1{P}", f"bidt{P}"], writes=[f"rep{P}"])
            op("vector", lambda S=S: EN["vector"].tensor_single_scalar(
                S["eqA"][:], S["rep"][:], -1e29, ALU.is_lt),
                reads=[f"rep{P}"], writes=[f"eqA{P}"])
            op("vector", lambda S=S: EN["vector"].tensor_single_scalar(
                S["pad2"][:], ones8[:], S["t8"][:, 0:1], ALU.mult),
                reads=["ones8", f"t8{P}"], writes=[f"pad2{P}"])
            op("vector", lambda S=S: EN["vector"].tensor_copy(S["pad2"][:, 1:2], S["t8"][:, 1:2]),
               reads=[f"t8{P}"], writes=[f"pad2{P}"])
            op("vector", lambda S=S: EN["vector"].match_replace(
                S["rep"][:], S["pad2"][:, :], S["bidt"][:], -1e30),
                reads=[f"pad2{P}", f"bidt{P}"], writes=[f"rep{P}"])
            op("vector", lambda S=S: EN["vector"].tensor_single_scalar(
                S["eqB"][:], S["rep"][:], -1e29, ALU.is_lt),
                reads=[f"rep{P}"], writes=[f"eqB{P}"])
            op("vector", lambda S=S: EN["vector"].tensor_sub(S["d12"][:], S["t8"][:, 0:1], S["t8"][:, 1:2]),
               reads=[f"t8{P}"], writes=[f"d12{P}"])
            op("scalar", lambda S=S: EN["scalar"].activation(S["r1"][:, 0:1], S["d12"][:, 0:1], AF.Sigmoid),
               reads=[f"d12{P}"], writes=[f"r1{P}"])
            op("scalar", lambda S=S: EN["scalar"].activation(
                S["r2"][:, 0:1], S["r1"][:, 0:1], AF.Copy, scale=-1.0, bias=1.0),
                reads=[f"r1{P}"], writes=[f"r2{P}"])
            op("vector", lambda S=S: EN["vector"].scalar_tensor_tensor(
                S["wal"][:], S["eqA"][:], S["r1"][:, 0:1], S["eqA"][:], ALU.mult, ALU.bypass),
                reads=[f"eqA{P}", f"r1{P}"], writes=[f"wal{P}"])
            op("vector", lambda S=S: EN["vector"].tensor_sub(S["eqB"][:], S["eqB"][:], S["eqA"][:]),
               reads=[f"eqB{P}", f"eqA{P}"], writes=[f"eqB{P}"])
            op("vector", lambda S=S: EN["vector"].scalar_tensor_tensor(
                S["wal"][:], S["eqB"][:], S["r2"][:, 0:1], S["wal"][:], ALU.mult, ALU.add),
                reads=[f"eqB{P}", f"r2{P}", f"wal{P}"], writes=[f"wal{P}"])
            if "weall" in taps:
                op("sync", lambda tile=tile, S=S: EN["sync"].dma_start(
                    tap_h["weall"][tile * 128:(tile + 1) * 128, :], S["wal"][:]),
                    reads=[f"wal{P}"], writes=["ser_ta" if tile % 2 == 0 else "ser_tb"],
                    inc="dtapa" if tile % 2 == 0 else "dtapb")
            op("tensor", lambda S=S, UB=UB: EN["tensor"].transpose(
                PS[0:E, UB + 512:UB + 640], S["wal"][:], ident_f[:]),
                reads=[f"wal{P}", "ident_f"], writes=psk(UB + 512, UB + 640), inc=True)
            op("vector", lambda S=S, UB=UB: EN["vector"].tensor_copy(
                S["wet"][:, :], PS[0:E, UB + 512:UB + 640]),
                reads=psk(UB + 512, UB + 640), writes=[f"wet{P}"])
            for eh in range(2):
                for j in range(4):
                    e = eh * 4 + j
                    c0 = UB + eh * 512 + j * 128
                    op("tensor", lambda e=e, c0=c0, S=S: EN["tensor"].matmul(
                        PS[:, c0:c0 + 128], ohc[:, e * 128:(e + 1) * 128], S["wet"][:, :],
                        start=True, stop=True),
                        reads=["ohc", f"wet{P}"], writes=psk(c0, c0 + 128),
                        inc=(j == 3))
                op("vector", lambda eh=eh, ttl=ttl, UB=UB: EN["vector"].tensor_copy(
                    webc[:].rearrange("p (e c) -> p e c", c=1024)
                        [:, eh * 4:eh * 4 + 4, ttl * 128:(ttl + 1) * 128],
                    PS[:, UB + eh * 512: UB + eh * 512 + 512]
                        .rearrange("p (e c) -> p e c", c=128)),
                    reads=psk(UB + eh * 512, UB + eh * 512 + 512),
                    writes=[f"webc_{eh * 4 + j}" for j in range(4)])

        def emit_stage_a_group(g):
            segs = [sa_seg0, sa_seg1, sa_seg2, sa_seg3, sa_seg4]
            tiles = [g * TT_G + i for i in range(TT_G)]
            for r in range(2 * len(tiles) + len(segs)):
                for idx, t in enumerate(tiles):
                    j = r - 2 * idx
                    if 0 <= j < len(segs):
                        segs[j](g, t)

        def emit_xga(g):
            for e in range(E):
                buf, bk = wbuf[e % 3], f"wbuf{e % 3}"
                op("gpsimd", lambda e=e, buf=buf: EN["gpsimd"].dma_start(
                    buf[:, 0:KT * 128], xau_h[e]), writes=[bk], inc="dwb%d" % (e % 3))
                for kt in range(KT):
                    for fc in range(2):
                        c0 = 2048 + fc * 512
                        op("tensor", lambda kt=kt, fc=fc, buf=buf, c0=c0: EN["tensor"].matmul(
                            PS[:, c0:c0 + 512], buf[:, kt * 128:(kt + 1) * 128],
                            ln1t[:, kt * 1024 + fc * 512: kt * 1024 + (fc + 1) * 512],
                            start=(kt == 0), stop=(kt == KT - 1)),
                            reads=[bk, f"ln1t_{kt}"], writes=psk(c0, c0 + 512),
                            inc=(kt == KT - 1))
                sg_, su_ = e, 8 + e
                sgn, sun = (e + 1) % 8, 8 + (e + 1) % 8
                op("vector", lambda sg_=sg_: EN["vector"].tensor_single_scalar(
                    xga[0:64, sg_ * 1024:(sg_ + 1) * 1024], PS[0:64, 2048:3072],
                    SCALING, ALU.mult),
                    reads=psk(2048, 3072), writes=[f"xga_{sg_}a"])
                op("vector", lambda su_=su_: EN["vector"].tensor_single_scalar(
                    xga[0:64, su_ * 1024:(su_ + 1) * 1024], PS[64:128, 2048:3072],
                    SCALING, ALU.mult),
                    reads=psk(2048, 3072), writes=[f"xga_{su_}a"])
                if e < E - 1:
                    op("vector", lambda sgn=sgn: EN["vector"].tensor_single_scalar(
                        xga[64:128, sgn * 1024:(sgn + 1) * 1024], PS[0:64, 2048:3072],
                        -SCALING, ALU.mult),
                        reads=psk(2048, 3072), writes=[f"xga_{sgn}b"])
                    op("vector", lambda sun=sun: EN["vector"].tensor_single_scalar(
                        xga[64:128, sun * 1024:(sun + 1) * 1024], PS[64:128, 2048:3072],
                        -SCALING, ALU.mult),
                        reads=psk(2048, 3072), writes=[f"xga_{sun}b"])
            op("vector", lambda: EN["vector"].memset(xga[64:128, 0:1024], 0.0),
               writes=["xga_0b"])
            op("vector", lambda: EN["vector"].memset(xga[64:128, 8 * 1024:9 * 1024], 0.0),
               writes=["xga_8b"])

        def p1_regions(it):
            if it % 2 == 0:
                return 0, 1024
            return 2048, 3072

        def emit_p1_dma(g, it):
            buf, bk = wbuf[it % 3], f"wbuf{it % 3}"
            op("gpsimd", lambda it=it, buf=buf: EN["gpsimd"].dma_start(
                buf[:], wslab_h[it]), writes=[bk], inc="dwb%d" % (it % 3))

        def emit_p1_base_chunk(g, it, which, kt):
            buf, bk = wbuf[it % 3], f"wbuf{it % 3}"
            G0, U0 = p1_regions(it)
            base, col = (WC_BG, G0) if which == "g" else (WC_BU, U0)
            for fc in range(2):
                op("tensor", lambda kt=kt, fc=fc, buf=buf, base=base, col=col:
                    EN["tensor"].matmul(
                        PS[:, col + fc * 512: col + (fc + 1) * 512],
                        buf[:, base + kt * 128: base + (kt + 1) * 128],
                        ln1t[:, kt * 1024 + fc * 512: kt * 1024 + (fc + 1) * 512],
                        start=(kt == 0), stop=False, skip_group_check=True),
                    reads=[bk, f"ln1t_{kt}"], writes=[f"B{(col + fc * 512) // 512}"],
                    inc=False)

        def emit_p1_expert(g, it, e):
            buf, bk = wbuf[it % 3], f"wbuf{it % 3}"
            G0, U0 = p1_regions(it)
            if e == 0:
                op("scalar", lambda buf=buf, it=it: EN["scalar"].copy(
                    dabuf[it % 2][:], buf[:, WC_DA:WC_DA + E * R]),
                    reads=[bk], writes=[f"dabuf{it % 2}"])
            sg_, su_ = e, 8 + e
            for fc in range(2):
                op("tensor", lambda e=e, fc=fc, buf=buf, sg_=sg_, G0=G0:
                    EN["tensor"].matmul(
                        PS[:, G0 + fc * 512: G0 + (fc + 1) * 512],
                        buf[:, WC_GB + e * 128: WC_GB + (e + 1) * 128],
                        xga[:, sg_ * 1024 + fc * 512: sg_ * 1024 + (fc + 1) * 512],
                        start=False, stop=(e == E - 1), skip_group_check=True),
                    reads=[bk, f"xga_{sg_}a", f"xga_{sg_}b"],
                    writes=[f"B{(G0 + fc * 512) // 512}"], inc=True)
                op("tensor", lambda e=e, fc=fc, buf=buf, su_=su_, U0=U0:
                    EN["tensor"].matmul(
                        PS[:, U0 + fc * 512: U0 + (fc + 1) * 512],
                        buf[:, WC_UB + e * 128: WC_UB + (e + 1) * 128],
                        xga[:, su_ * 1024 + fc * 512: su_ * 1024 + (fc + 1) * 512],
                        start=False, stop=(e == E - 1), skip_group_check=True),
                    reads=[bk, f"xga_{su_}a", f"xga_{su_}b"],
                    writes=[f"B{(U0 + fc * 512) // 512}"], inc=True)
            sg2, ub2 = sgb[e % 2], ubb[e % 2]
            sgk, ubk = f"sgb{e % 2}", f"ubb{e % 2}"
            op("scalar", lambda sg2=sg2, G0=G0: EN["scalar"].activation(
                sg2[:], PS[:, G0:G0 + 1024], AF.Silu),
                reads=psk(G0, G0 + 1024), writes=[sgk])
            op("scalar", lambda ub2=ub2, U0=U0: EN["scalar"].copy(
                ub2[:], PS[:, U0:U0 + 1024]),
                reads=psk(U0, U0 + 1024), writes=[ubk])
            op("vector", lambda e=e, sg2=sg2, ub2=ub2: EN["vector"].tensor_mul(
                h8[:, e * 1024:(e + 1) * 1024], sg2[:], ub2[:]),
                reads=[sgk, ubk], writes=[f"h8_{e}"])
            hwv, hwk = hmixw[it % 2], f"hmixw{it % 2}"
            if e == 0:
                op("vector", lambda hwv=hwv: EN["vector"].tensor_mul(
                    hwv[:], h8[:, 0:1024], webc[:, 0:1024]),
                    reads=["h8_0", "webc_0"], writes=[hwk])
            else:
                op("vector", lambda e=e, sg2=sg2: EN["vector"].tensor_mul(
                    sg2[:], h8[:, e * 1024:(e + 1) * 1024],
                    webc[:, e * 1024:(e + 1) * 1024]),
                    reads=[f"h8_{e}", f"webc_{e}"], writes=[sgk])
                op("vector", lambda hwv=hwv, sg2=sg2: EN["vector"].tensor_add(
                    hwv[:], hwv[:], sg2[:]), reads=[hwk, sgk], writes=[hwk])

        def emit_p1_hmix_store(g, it):
            hwv, hwk = hmixw[it % 2], f"hmixw{it % 2}"
            op("sync", lambda it=it, g=g, hwv=hwv: EN["sync"].dma_start(
                hx_h[it, :, g * TG:(g + 1) * TG], hwv[:]),
                reads=[hwk], writes=[f"hbm_hx_{it}_{g}"], inc="dhx%d" % (it % 2))
            if "hmix" in taps:
                op("sync", lambda it=it, g=g, hwv=hwv: EN["sync"].dma_start(
                    tap_h["hmix"][it, :, g * TG:(g + 1) * TG], hwv[:]),
                    reads=[hwk], writes=["ser_ta" if it % 2 == 0 else "ser_tb"],
                    inc="dtapa" if it % 2 == 0 else "dtapb")

        def emit_p1_dA_pair(g, it, pr):
            dab, dak = dabuf[it % 2], f"dabuf{it % 2}"
            G0r, U0r = p1_regions(it)
            G0 = G0r if pr < 2 else U0r
            for hilo in range(2):
                e = 2 * pr + hilo
                for fc in range(2):
                    op("tensor", lambda e=e, fc=fc, hilo=hilo, dab=dab, G0=G0:
                        EN["tensor"].matmul(
                            PS[hilo * 64:(hilo + 1) * 64, G0 + fc * 512: G0 + (fc + 1) * 512],
                            dab[:, e * 64:(e + 1) * 64],
                            h8[:, e * 1024 + fc * 512: e * 1024 + (fc + 1) * 512],
                            start=True, stop=True),
                        reads=[dak, f"h8_{e}"],
                        writes=[f"B{(G0 + fc * 512) // 512}"], inc=True)
            for fc in range(2):
                c0 = G0 + fc * 512
                hs0 = pr * 1024 + fc * 512
                if it == 0:
                    op("vector", lambda c0=c0, hs0=hs0: EN["vector"].tensor_single_scalar(
                        hab[:, hs0:hs0 + 512], PS[:, c0:c0 + 512], SCALING, ALU.mult),
                        reads=psk(c0, c0 + 512), writes=[f"hab_{pr}_{fc}"])
                else:
                    op("vector", lambda c0=c0, hs0=hs0: EN["vector"].scalar_tensor_tensor(
                        hab[:, hs0:hs0 + 512], PS[:, c0:c0 + 512], SCALING,
                        hab[:, hs0:hs0 + 512], ALU.mult, ALU.add),
                        reads=psk(c0, c0 + 512) + [f"hab_{pr}_{fc}"],
                        writes=[f"hab_{pr}_{fc}"])

        def emit_phase1_group(g):
            emit_p1_dma(g, 0)
            emit_p1_dma(g, 1)
            for kt in range(KT):
                emit_p1_base_chunk(g, 0, "u", kt)
            for kt in range(KT):
                emit_p1_base_chunk(g, 0, "g", kt)
            for it in range(NIT):
                if it + 2 < NIT:
                    emit_p1_dma(g, it + 2)
                for e in range(E):
                    if it > 0 and e < 4:
                        emit_p1_dA_pair(g, it - 1, (0, 2, 1, 3)[e])
                    if it + 1 < NIT:
                        if e in (2, 3):
                            for kt in range(8 * (e - 2), 8 * (e - 2) + 8):
                                emit_p1_base_chunk(g, it + 1, "g", kt)
                        elif e >= 4:
                            for kt in range(4 * (e - 4), 4 * (e - 4) + 4):
                                emit_p1_base_chunk(g, it + 1, "u", kt)
                    emit_p1_expert(g, it, e)
                emit_p1_hmix_store(g, it)
            for pr in range(4):
                emit_p1_dA_pair(g, NIT - 1, pr)

        def emit_haw(g):
            for pr in range(4):
                for hilo in range(2):
                    e = 2 * pr + hilo
                    op("vector", lambda pr=pr, hilo=hilo, e=e: EN["vector"].tensor_mul(
                        h8[hilo * 64:(hilo + 1) * 64, pr * 1024:(pr + 1) * 1024],
                        hab[hilo * 64:(hilo + 1) * 64, pr * 1024:(pr + 1) * 1024],
                        webc[hilo * 64:(hilo + 1) * 64, e * 1024:(e + 1) * 1024]),
                        reads=[f"hab_{pr}_0", f"hab_{pr}_1", f"webc_{e}"],
                        writes=[f"hawstg_{pr}", f"h8_{pr}"])
            for pr in range(4):
                op("sync", lambda pr=pr, g=g: EN["sync"].dma_start(
                    haw_h[:, pr, g, :], h8[:, pr * 1024:(pr + 1) * 1024]),
                    reads=[f"hawstg_{pr}", f"h8_{pr}"],
                    writes=[f"hbm_haw_{g}"],
                    inc="dhw0" if g == 0 else "dhw1")

        def emit_phase2():
            GA = 7
            NG = NIT // GA
            op("sync", lambda: EN["sync"].dma_start(
                hawb[:], haw_h.rearrange("p a b c -> p (a b c)")),
                reads=["hbm_haw_0", "hbm_haw_1"], writes=["hawball"], inc="dhg0")
            for dq in range(4):
                for gi in range(NG):
                    buf, bk = wbuf[gi % 2], f"wbuf{gi % 2}"
                    hg, hk = hmg[gi % 2], f"hmg{gi % 2}"
                    op("gpsimd", lambda dq=dq, gi=gi, buf=buf: EN["gpsimd"].dma_start(
                        buf[:, 0:GA * 512].rearrange("p (g c) -> p g c", g=GA),
                        bd_h[dq, gi * GA:(gi + 1) * GA].rearrange("g p c -> p g c")),
                        writes=[bk + "bd"], inc="dwb%d" % (gi % 2))
                    for j in range(GA):
                        it = gi * GA + j
                        rd = [f"hbm_hx_{it}_{gg}" for gg in range(NGRP)]
                        op("sync", lambda j=j, it=it, hg=hg: EN["sync"].dma_start(
                            hg[:, j * 2048:(j + 1) * 2048], hx_h[it, :, :]),
                            reads=rd, writes=[hk + f"x{j}"], inc="dhg%d" % (gi % 2))
                    tk.mark_done([hk + f"x{j}" for j in range(GA)], "dhg%d" % (gi % 2))
                    for dtl in range(4):
                        pc = 0 if dtl % 2 == 0 else 2048
                        for j in range(GA):
                            for fc in range(4):
                                op("tensor", lambda dtl=dtl, j=j, fc=fc, buf=buf, hg=hg, pc=pc:
                                    EN["tensor"].matmul(
                                        PS[:, pc + fc * 512: pc + (fc + 1) * 512],
                                        buf[:, j * 512 + dtl * 128: j * 512 + (dtl + 1) * 128],
                                        hg[:, j * 2048 + fc * 512: j * 2048 + (fc + 1) * 512],
                                        start=(j == 0), stop=(j == GA - 1)),
                                    reads=[bk + "bd", hk + f"x{j}"],
                                    writes=[f"B{(pc + fc * 512) // 512}"],
                                    inc=(j == GA - 1))
                        if gi == 0:
                            op("vector", lambda dtl=dtl, pc=pc: EN["vector"].tensor_copy(
                                outacc[:, dtl * 2048:(dtl + 1) * 2048], PS[:, pc:pc + 2048]),
                                reads=psk(pc, pc + 2048), writes=[f"oa_{dtl}"])
                        else:
                            op("vector", lambda dtl=dtl, pc=pc: EN["vector"].tensor_add(
                                outacc[:, dtl * 2048:(dtl + 1) * 2048],
                                outacc[:, dtl * 2048:(dtl + 1) * 2048], PS[:, pc:pc + 2048]),
                                reads=psk(pc, pc + 2048) + [f"oa_{dtl}"], writes=[f"oa_{dtl}"])
                op("gpsimd", lambda dq=dq: EN["gpsimd"].dma_start(
                    wbuf[0][:, 0:4 * 512].rearrange("p (q c) -> p q c", q=4),
                    dbcat_h[:, :, dq * 512:(dq + 1) * 512].rearrange("q p c -> p q c")),
                    writes=["wbuf0bd"], inc="dwb0")
                for dtl in range(4):
                    pc = 0 if dtl % 2 == 0 else 2048
                    for pr in range(4):
                        for fc in range(4):
                            hkey = ["hawball"]
                            op("tensor", lambda dtl=dtl, pr=pr, fc=fc, pc=pc:
                                EN["tensor"].matmul(
                                    PS[:, pc + fc * 512: pc + (fc + 1) * 512],
                                    wbuf[0][:, pr * 512 + dtl * 128: pr * 512 + (dtl + 1) * 128],
                                    hawb[:, pr * TC + fc * 512: pr * TC + (fc + 1) * 512],
                                    start=(pr == 0), stop=(pr == 3)),
                                reads=["wbuf0bd"] + hkey,
                                writes=[f"B{(pc + fc * 512) // 512}"],
                                inc=(pr == 3))
                    op("vector", lambda dtl=dtl, pc=pc: EN["vector"].tensor_add(
                        outacc[:, dtl * 2048:(dtl + 1) * 2048],
                        outacc[:, dtl * 2048:(dtl + 1) * 2048], PS[:, pc:pc + 2048]),
                        reads=psk(pc, pc + 2048) + [f"oa_{dtl}"], writes=[f"oa_{dtl}"])
                def x1load(tt, dq=dq):
                    xb, xk = x1buf[tt % 4], f"x1buf{tt % 4}"
                    op("sync", lambda tt=tt, dq=dq, xb=xb: EN["sync"].dma_start(
                        xb[:], x1s_h[tt, :, dq * 512:(dq + 1) * 512]),
                        reads=[f"hbm_x1_{tt}"], writes=[xk], inc="dxb%d" % (tt % 4))
                x1load(0)
                x1load(1)
                for tt in range(NTT):
                    xb, xk = x1buf[tt % 4], f"x1buf{tt % 4}"
                    orow, ork = outrow[tt % 4], f"outrow{tt % 4}"
                    if tt + 2 < NTT:
                        x1load(tt + 2)
                    for dtl in range(4):
                        slot = 3072 + dtl * 128
                        op("tensor", lambda dtl=dtl, tt=tt, slot=slot: EN["tensor"].transpose(
                            PS[:, slot:slot + 128],
                            outacc[:, dtl * 2048 + tt * 128: dtl * 2048 + (tt + 1) * 128],
                            ident_f[:]),
                            reads=[f"oa_{dtl}", "ident_f"], writes=psk(slot, slot + 128),
                            inc=(dtl == 3))
                    op("vector", lambda xb=xb, orow=orow: EN["vector"].tensor_add(
                        orow[:], PS[:, 3072:3584], xb[:]),
                        reads=psk(3072, 3584) + [xk], writes=[ork])
                    op("sync", lambda tt=tt, dq=dq, orow=orow: EN["sync"].dma_start(
                        y_h[tt * 128:(tt + 1) * 128, dq * 512:(dq + 1) * 512], orow[:]),
                        reads=[ork], writes=["hbm_y"], inc="dya" if tt % 2 == 0 else "dyb")

        def build_all():
            stage = [0]

            def go():
                stage[0] += 1
                return stop_after is None or stage[0] <= stop_after
            emit_init()
            done = False
            for g in range(NGRP):
                if g > 0:
                    tk.barrier()
                if go():
                    emit_stage_a_group(g)
                else:
                    done = True
                if not done and go():
                    emit_xga(g)
                else:
                    done = True
                tk.barrier()
                if not done and go():
                    emit_phase1_group(g)
                else:
                    done = True
                if not done and go():
                    emit_haw(g)
                else:
                    done = True
            tk.barrier()
            if not done and go():
                emit_phase2()
            tk.barrier()

        for engname in ENGS:
            def body(engobj, engname=engname):
                EN.clear()
                EN[engname] = engobj
                tk.reset(engname, engobj)
                build_all()
            getattr(blk, engname)(body)

    return nc, sorted(taps)



def _prep_inputs(L0_attn_u, L0_attn_A_factors, L0_attn_W_b_rank, L0_attn_W_b_state,
                 L0_attn_W_c_rank, L0_attn_W_c_model, L0_attn_D_vec,
                 L0_ffn_expert_wealth, L0_ffn_conf_W, L0_ffn_conf_b,
                 L0_ffn_base_gate, L0_ffn_base_up, L0_ffn_base_down,
                 L0_ffn_gate_A, L0_ffn_gate_B, L0_ffn_up_A, L0_ffn_up_B,
                 L0_ffn_down_A, L0_ffn_down_B):
    f = lambda a: np.ascontiguousarray(np.asarray(a), dtype=np.float32)
    u = f(L0_attn_u).reshape(NCORES * TC, D)
    bg = f(L0_ffn_base_gate).reshape(KT, 128, NIT, 128).transpose(2, 1, 0, 3).reshape(NIT, 128, KT * 128)
    bu = f(L0_ffn_base_up).reshape(KT, 128, NIT, 128).transpose(2, 1, 0, 3).reshape(NIT, 128, KT * 128)

    def gbcat(W):
        Wt = f(W).reshape(E, R, NIT, 128)
        cur = Wt.transpose(2, 1, 0, 3)
        prv = Wt[(np.arange(E) - 1) % E].transpose(2, 1, 0, 3)
        return np.concatenate([cur, prv], axis=1).reshape(NIT, 128, E * 128)

    dA = f(L0_ffn_down_A).reshape(E, NIT, 128, R).transpose(1, 2, 0, 3).reshape(NIT, 128, E * R)
    wslab = np.ascontiguousarray(
        np.concatenate([bg, bu, gbcat(L0_ffn_gate_B), gbcat(L0_ffn_up_B), dA], axis=2))
    gA = f(L0_ffn_gate_A).reshape(E, KT, 128, R)
    uA = f(L0_ffn_up_A).reshape(E, KT, 128, R)
    xau = np.concatenate([gA, uA], axis=3)
    xau = np.ascontiguousarray(xau.transpose(0, 2, 1, 3).reshape(E, 128, KT * 128))
    bd = np.ascontiguousarray(
        f(L0_ffn_base_down).reshape(NIT, 128, 4, 512).transpose(2, 0, 1, 3))
    dbcat = np.ascontiguousarray(f(L0_ffn_down_B).reshape(4, 2 * R, D))
    return {
        "u": u,
        "wslab": wslab,
        "xau": xau,
        "bdslab": bd,
        "dbcat": dbcat,
        "wbr": f(L0_attn_W_b_rank),
        "wbs": f(L0_attn_W_b_state),
        "wcr": f(L0_attn_W_c_rank),
        "wcm": f(L0_attn_W_c_model),
        "afac": f(L0_attn_A_factors).reshape(1, 32),
        "dvecr": np.ascontiguousarray(f(L0_attn_D_vec).reshape(KT, 128).T),
        "confwt": np.ascontiguousarray(f(L0_ffn_conf_W).T),
        "confb": f(L0_ffn_conf_b).reshape(1, E),
        "wealth": f(L0_ffn_expert_wealth).reshape(1, E),
    }


def kernel(**inputs):
    ins = _prep_inputs(**inputs)
    nc, _ = build_core_program()
    u = ins.pop("u")
    in_maps = []
    for c in range(NCORES):
        m = dict(ins)
        m["u"] = np.ascontiguousarray(u[c * TC:(c + 1) * TC])
        in_maps.append(m)
    res = run_bass_kernel_spmd(nc, in_maps, core_ids=list(range(NCORES)))
    out = np.concatenate([r["y"] for r in res.results], axis=0)
    return out.reshape(2, 8192, 2048)


# version 16
# speedup vs baseline: 1.0802x; 1.0802x over previous
import numpy as np
from contextlib import ExitStack

import concourse.bass as bass
import concourse.mybir as mybir
from concourse.bass_utils import run_bass_kernel_spmd

F32 = mybir.dt.float32
BF16 = mybir.dt.bfloat16
AF = mybir.ActivationFunctionType
ALU = mybir.AluOpType
AX = mybir.AxisListType

NCORES = 8
TC = 2048
NGRP = 2
TG = TC // NGRP
NTT = TC // 128
TT_G = TG // 128
D = 2048
KT = D // 128
I = 7168
NIT = I // 128
E = 8
R = 64
NST = 256
LEV = 8
EPS = 1e-5
SCALING = 16.0 / 64.0

WC_BG = 0
WC_BU = WC_BG + KT * 128
WC_GB = WC_BU + KT * 128
WC_UB = WC_GB + E * 128
WC_DA = WC_UB + E * 128
WCOLS = WC_DA + E * 64

ENGS = ("gpsimd", "scalar", "vector", "sync", "tensor")
SA_LIMIT = 99
DSEMS = ("dhw0", "dhw1", "dwb0", "dwb1", "dwb2", "dhg0", "dhg1", "dx0", "dx1a", "dx1b", "dhx0",
         "dhx1", "dxb0", "dxb1", "dxb2", "dxb3", "dya", "dyb", "dini", "dtapa", "dtapb")


class Trk:

    def __init__(self, nc, sems):
        self.nc = nc
        self.sem = sems

    def reset(self, current, engobj):
        self.current = current
        self.engobj = engobj
        self.cnt = {e: 0 for e in ENGS}
        for s in DSEMS:
            self.cnt[s] = 0
        self.seq = {e: 0 for e in ENGS}
        self.incseqs = {e: [] for e in ENGS}
        self.w = {}
        self.r = {}
        self.waited = {}

    def _wait(self, semname, count):
        if count <= self.waited.get(semname, 0):
            return
        self.waited[semname] = count
        self.engobj.wait_ge(self.sem[semname], count)

    def _resolve(self, k, tok):
        semname, count, seq = tok
        if count is not None:
            return semname, count
        import bisect
        lst = self.incseqs[semname]
        idx = bisect.bisect_left(lst, seq)
        if idx >= len(lst):
            raise AssertionError(
                f"dep on un-inc'd {semname} op (key={k}) with no later inc yet")
        return semname, idx + 1

    def op(self, eng, fn, reads=(), writes=(), inc=True):
        deps = []
        for k in reads:
            if k in self.w:
                deps.append((k, self.w[k]))
        for k in writes:
            if k in self.w:
                deps.append((k, self.w[k]))
            for tok in self.r.get(k, {}).values():
                deps.append((k, tok))
        need = {}
        for k, tok in deps:
            if tok[0] == eng == "tensor":
                continue
            semname, count = self._resolve(k, tok)
            need[semname] = max(need.get(semname, 0), count)
        dma = isinstance(inc, str)
        if eng == self.current:
            for semname in sorted(need):
                self._wait(semname, need[semname])
            ins = fn()
        else:
            ins = None
        myseq = self.seq[eng]
        self.seq[eng] += 1
        if dma:
            semname = inc
            self.cnt[semname] += 16
            if ins is not None:
                ins.then_inc(self.sem[semname], 16)
            tok = (semname, self.cnt[semname], myseq)
        elif inc:
            self.cnt[eng] += 1
            self.incseqs[eng].append(myseq)
            if ins is not None:
                ins.then_inc(self.sem[eng], 1)
            tok = (eng, self.cnt[eng], myseq)
        else:
            tok = (eng, None, myseq)
        for k in writes:
            self.w[k] = tok
            self.r[k] = {}
        for k in reads:
            self.r.setdefault(k, {})[tok[0]] = tok
        return ins

    def barrier(self):
        for semname in list(ENGS) + list(DSEMS):
            c = self.cnt[semname]
            if c > 0 and semname != self.current:
                self._wait(semname, c)

    def mark_done(self, keys, semname):
        tok = (semname, self.cnt[semname], 0)
        for k in keys:
            if k in self.w:
                self.w[k] = tok


def psk(c0, c1):
    return [f"B{b}" for b in range(c0 // 512, (c1 + 511) // 512)]


def build_core_program(taps=(), stop_after=None):
    nc = bass.Bass("TRN2", target_bir_lowering=False, debug=False)
    taps = set(taps)

    u_h = nc.dram_tensor("u", [TC, D], F32, kind="ExternalInput").ap()
    wslab_h = nc.dram_tensor("wslab", [NIT, 128, WCOLS], F32, kind="ExternalInput").ap()
    xau_h = nc.dram_tensor("xau", [E, 128, KT * 128], F32, kind="ExternalInput").ap()
    bd_h = nc.dram_tensor("bdslab", [4, NIT, 128, 512], F32, kind="ExternalInput").ap()
    dbcat_h = nc.dram_tensor("dbcat", [4, 128, D], F32, kind="ExternalInput").ap()
    wbr_h = nc.dram_tensor("wbr", [D, R], F32, kind="ExternalInput").ap()
    wbs_h = nc.dram_tensor("wbs", [R, NST], F32, kind="ExternalInput").ap()
    wcr_h = nc.dram_tensor("wcr", [NST, R], F32, kind="ExternalInput").ap()
    wcm_h = nc.dram_tensor("wcm", [R, D], F32, kind="ExternalInput").ap()
    afac_h = nc.dram_tensor("afac", [1, 32], F32, kind="ExternalInput").ap()
    dvec_h = nc.dram_tensor("dvecr", [128, KT], F32, kind="ExternalInput").ap()
    cwt_h = nc.dram_tensor("confwt", [D, E], F32, kind="ExternalInput").ap()
    cb_h = nc.dram_tensor("confb", [1, E], F32, kind="ExternalInput").ap()
    wl_h = nc.dram_tensor("wealth", [1, E], F32, kind="ExternalInput").ap()
    y_h = nc.dram_tensor("y", [TC, D], F32, kind="ExternalOutput").ap()
    x1s_h = nc.dram_tensor("x1scr", [NTT, 128, D], F32, kind="Internal").ap()
    hx_h = nc.dram_tensor("hmixscr", [NIT, 128, TC], BF16, kind="Internal").ap()
    haw_h = nc.dram_tensor("hawscr", [128, 4, NGRP, TG], BF16, kind="Internal").ap()
    tap_h = {}
    if "x1" in taps:
        tap_h["x1"] = nc.dram_tensor("tap_x1", [TC, D], F32, kind="ExternalOutput").ap()
    if "weall" in taps:
        tap_h["weall"] = nc.dram_tensor("tap_weall", [TC, E], F32, kind="ExternalOutput").ap()
    if "hmix" in taps:
        tap_h["hmix"] = nc.dram_tensor("tap_hmix", [NIT, 128, TC], BF16, kind="ExternalOutput").ap()
    if "lnt" in taps:
        tap_h["lnt"] = nc.dram_tensor("tap_lnt", [NTT, 128, D], F32, kind="ExternalOutput").ap()

    es = ExitStack()
    sb = lambda n, shape, dt: es.enter_context(nc.sbuf_tensor(n, shape, dt))
    with es:
        sems = {}
        for e_ in ENGS:
            sems[e_] = es.enter_context(nc.semaphore(f"sem_{e_}"))
        for s_ in DSEMS:
            sems[s_] = es.enter_context(nc.semaphore(f"semd_{s_}"))
        tk = Trk(nc, sems)

        wbuf = [sb(f"wbuf{i}", [128, WCOLS], BF16) for i in range(3)]
        wbr = sb("wbr_s", [128, KT * R], F32)
        wcm = sb("wcm_s", [R, D], F32)
        wcr = sb("wcr_s", [128, 2 * R], F32)
        wbsA = sb("wbsA", [R, NST], F32)
        wbsB = sb("wbsB", [R, NST], F32)
        wbsT = sb("wbsT", [128, 2 * R], F32)
        wmid = sb("wmid", [R, R], F32)
        afb = sb("afb", [R, 32], F32)
        cwt = sb("cwt_s", [128, KT * E], F32)
        cbb = sb("cbb", [128, E], F32)
        wlb = sb("wlb", [128, E], F32)
        dvr = sb("dvr", [128, KT], F32)
        ident_f = sb("ident_f", [128, 128], F32)
        ones1 = sb("ones1", [1, 128], F32)
        ones8 = sb("ones8", [128, E], F32)
        ohc = sb("ohc", [E, E * 128], F32)
        epsb = sb("epsb", [128, 1], F32)
        ph1 = ExitStack()
        sb1 = lambda n, shape, dt: ph1.enter_context(nc.sbuf_tensor(n, shape, dt))
        ln1t = sb1("ln1t", [128, KT * 1024], BF16)
        xga = sb1("xga", [128, 16 * 1024], BF16)
        webc = sb1("webc", [128, E * 1024], BF16)
        hab = sb1("hab", [128, 4 * 1024], F32)
        saes = ExitStack()
        sbA = lambda n, shape, dt: saes.enter_context(nc.sbuf_tensor(n, shape, dt))
        x0t = [sbA(f"x0t{i}", [128, D], F32) for i in range(2)]
        lnt = [sbA(f"lnt{i}", [128, D], F32) for i in range(2)]
        sa = []
        for i in range(2):
            sa.append({nm: sbA(f"{nm}{i}", [128, E], F32)
                       for nm in ("t8", "pad1", "pad2", "rep", "eqA", "eqB", "wal", "bidt")})
            for nm in ("d12", "r1", "r2", "mstat", "vstat", "sstat", "rstd"):
                sa[i][nm] = sbA(f"{nm}{i}", [128, 1], F32)
            sa[i]["wet"] = sbA(f"wet{i}", [E, 128], F32)
            sa[i]["s1b"] = sbA(f"s1b{i}", [R, 128], F32)
            sa[i]["s2b"] = sbA(f"s2b{i}", [R, 128], F32)
            sa[i]["c8b"] = sbA(f"c8b{i}", [E, 128], F32)
        saes.close()
        hmixw = [sb1(f"hmixw{i}", [128, 1024], BF16) for i in range(2)]
        sgb = [sb1(f"sgb{i}", [128, 1024], BF16) for i in range(2)]
        ubb = [sb1(f"ubb{i}", [128, 1024], BF16) for i in range(2)]
        h8 = sb1("h8", [128, E * 1024], BF16)
        dabuf = [sb1(f"dabuf{i}", [128, E * R], BF16) for i in range(2)]
        ph1.close()
        outacc = sb("outacc", [128, 4 * 2048], F32)
        hawb = sb("hawb2", [128, 4 * TC], BF16)
        hmg = [sb(f"hmg{i}", [128, 7 * 2048], BF16) for i in range(2)]
        x1buf = [sb(f"x1buf{i}", [128, 512], F32) for i in range(4)]
        outrow = [sb(f"outrow{i}", [128, 512], F32) for i in range(4)]

        PS = es.enter_context(nc.psum_tensor([128, 4096], F32))

        blk = es.enter_context(nc.Block())
        EN = {}

        def op(eng, fn, reads=(), writes=(), inc=True):
            return tk.op(eng, fn, reads=reads, writes=writes, inc=inc)

        def emit_init():
            op("gpsimd", lambda: EN["gpsimd"].iota(
                sa[0]["mstat"][:], [[1, 1]], base=0, channel_multiplier=1,
                allow_small_or_imprecise_dtypes=True), writes=["mstat0"])
            op("gpsimd", lambda: EN["gpsimd"].iota(
                lnt[0][:, 0:128], [[1, 128]], base=0, channel_multiplier=0,
                allow_small_or_imprecise_dtypes=True), writes=["lnt0_sl0"])
            op("vector", lambda: EN["vector"].tensor_single_scalar(
                ident_f[:], lnt[0][:, 0:128], sa[0]["mstat"][:, 0:1], ALU.is_equal),
                reads=["lnt0_sl0", "mstat0"], writes=["ident_f"])
            op("vector", lambda: EN["vector"].memset(ones1[:], 1.0), writes=["ones1"])
            op("vector", lambda: EN["vector"].memset(epsb[:], EPS), writes=["epsb"])
            op("vector", lambda: EN["vector"].memset(ones8[:], 1.0), writes=["ones8"])
            op("gpsimd", lambda: EN["gpsimd"].iota(
                ohc[:].rearrange("p (b c) -> p b c", b=E), [[1, E], [0, 128]],
                base=0, channel_multiplier=0,
                allow_small_or_imprecise_dtypes=True), writes=["ohc"])
            op("vector", lambda: EN["vector"].tensor_single_scalar(
                ohc[:], ohc[:], sa[0]["mstat"][0:E, 0:1], ALU.is_equal),
                reads=["ohc", "mstat0"], writes=["ohc"])
            op("sync", lambda: EN["sync"].dma_start(
                wbr[:].rearrange("p (k r) -> p k r", r=R),
                wbr_h.rearrange("(k p) r -> p k r", p=128)), writes=["wbr"], inc="dini")
            op("sync", lambda: EN["sync"].dma_start(wcm[:], wcm_h[:, :]), writes=["wcm"], inc="dini")
            op("sync", lambda: EN["sync"].dma_start(
                wcr[:].rearrange("p (k r) -> p k r", r=R),
                wcr_h.rearrange("(k p) r -> p k r", p=128)), writes=["wcr"], inc="dini")
            op("sync", lambda: EN["sync"].dma_start(wbsA[:], wbs_h[:, :]), writes=["wbsA"], inc="dini")
            op("sync", lambda: EN["sync"].dma_start(
                cwt[:].rearrange("p (k e) -> p k e", e=E),
                cwt_h.rearrange("(k p) e -> p k e", p=128)), writes=["cwt"], inc="dini")
            op("sync", lambda: EN["sync"].dma_start(afb[0:1, :], afac_h[:, :]), writes=["afb"], inc="dini")
            op("sync", lambda: EN["sync"].dma_start(cbb[0:1, :], cb_h[:, :]), writes=["cbb"], inc="dini")
            op("sync", lambda: EN["sync"].dma_start(wlb[0:1, :], wl_h[:, :]), writes=["wlb"], inc="dini")
            op("sync", lambda: EN["sync"].dma_start(dvr[:], dvec_h[:, :]), writes=["dvr"], inc="dini")
            tk.mark_done(["wbr", "wcm", "wcr", "wbsA", "cwt", "afb", "cbb", "wlb", "dvr"], "dini")
            op("tensor", lambda: EN["tensor"].matmul(
                PS[:, 3072:3104], ones1[:], afb[0:1, :], start=True, stop=True),
                reads=["ones1", "afb"], writes=psk(3072, 3104), inc=True)
            op("tensor", lambda: EN["tensor"].matmul(
                PS[:, 3104:3112], ones1[:], cbb[0:1, :], start=True, stop=True),
                reads=["ones1", "cbb"], writes=psk(3104, 3112), inc=True)
            op("tensor", lambda: EN["tensor"].matmul(
                PS[:, 3112:3120], ones1[:], wlb[0:1, :], start=True, stop=True),
                reads=["ones1", "wlb"], writes=psk(3112, 3120), inc=True)
            op("vector", lambda: EN["vector"].tensor_copy(afb[:, :], PS[0:R, 3072:3104]),
               reads=psk(3072, 3104), writes=["afb"])
            op("vector", lambda: EN["vector"].tensor_copy(cbb[:, :], PS[:, 3104:3112]),
               reads=psk(3104, 3112), writes=["cbb"])
            op("vector", lambda: EN["vector"].tensor_copy(wlb[:, :], PS[:, 3112:3120]),
               reads=psk(3112, 3120), writes=["wlb"])
            src, dst, srck, dstk = wbsA, wbsB, "wbsA", "wbsB"
            for lvl in range(LEV):
                w = 1 << (LEV - 1 - lvl)
                for j in range(2):
                    a0 = afb[0:R, (lvl * 4 + j):(lvl * 4 + j) + 1]
                    a1 = afb[0:R, (lvl * 4 + 2 + j):(lvl * 4 + 2 + j) + 1]
                    vsrc = src[:, 0:NST].rearrange("p (h two w) -> p h two w", two=2, w=w)
                    vdst = dst[:, 0:NST].rearrange("p (h two w) -> p h two w", two=2, w=w)
                    scr = lnt[0][0:R, 0:NST // 2].rearrange("p (h w) -> p h w", w=w)
                    op("vector", lambda vsrc=vsrc, a0=a0, scr=scr: EN["vector"].tensor_single_scalar(
                        scr, vsrc[:, :, 0, :], a0, ALU.mult),
                        reads=[srck, "afb"], writes=["lnt0_scr"] + [f"lnt0_sl{k}" for k in range(2)])
                    op("vector", lambda vsrc=vsrc, vdst=vdst, a1=a1, j=j, scr=scr:
                        EN["vector"].scalar_tensor_tensor(
                            vdst[:, :, j, :], vsrc[:, :, 1, :], a1, scr, ALU.mult, ALU.add),
                        reads=[srck, "afb", "lnt0_scr"], writes=[dstk])
                src, dst, srck, dstk = dst, src, dstk, srck
            for k in range(2):
                op("tensor", lambda k=k, src=src: EN["tensor"].transpose(
                    PS[:, 3072 + k * 512: 3072 + k * 512 + 64],
                    src[0:R, k * 128:(k + 1) * 128], ident_f[0:R, 0:R]),
                    reads=[srck, "ident_f"], writes=psk(3072 + k * 512, 3072 + k * 512 + 64),
                    inc=True)
            op("vector", lambda: EN["vector"].tensor_copy(wbsT[:, 0:R], PS[:, 3072:3136]),
               reads=psk(3072, 3136), writes=["wbsT"])
            op("vector", lambda: EN["vector"].tensor_copy(wbsT[:, R:2 * R], PS[:, 3584:3648]),
               reads=psk(3584, 3648), writes=["wbsT"])
            for k in range(2):
                op("tensor", lambda k=k: EN["tensor"].matmul(
                    PS[0:R, 3072:3136], wbsT[:, k * R:(k + 1) * R],
                    wcr[:, k * R:(k + 1) * R], start=(k == 0), stop=(k == 1)),
                    reads=["wbsT", "wcr"], writes=psk(3072, 3136), inc=(k == 1))
            op("vector", lambda: EN["vector"].tensor_copy(wmid[:, :], PS[0:R, 3072:3136]),
               reads=psk(3072, 3136), writes=["wmid"])

        def sa_layernorm(P, x0, ln, LNSL):
            S = sa[P]
            op("vector", lambda: EN["vector"].reduce_sum(S["sstat"][:], x0[:], axis=AX.X),
               reads=[f"x0t{P}"], writes=[f"sstat{P}"])
            op("scalar", lambda: EN["scalar"].activation(
                ln[:], x0[:], AF.Square, accum_out=S["vstat"][:]),
                reads=[f"x0t{P}"], writes=[f"vstat{P}", f"lnt{P}_scr"] + LNSL)
            op("vector", lambda: EN["vector"].tensor_scalar_mul(S["mstat"][:], S["sstat"][:], 1.0 / D),
               reads=[f"sstat{P}"], writes=[f"mstat{P}"])
            op("vector", lambda: EN["vector"].tensor_scalar_mul(S["vstat"][:], S["vstat"][:], 1.0 / D),
               reads=[f"vstat{P}"], writes=[f"vstat{P}"])
            op("vector", lambda: EN["vector"].scalar_tensor_tensor(
                S["rstd"][:], S["mstat"][:], 1.0, S["mstat"][:], ALU.mult, ALU.mult),
                reads=[f"mstat{P}"], writes=[f"rstd{P}"])
            op("vector", lambda: EN["vector"].tensor_sub(S["vstat"][:], S["vstat"][:], S["rstd"][:]),
               reads=[f"vstat{P}", f"rstd{P}"], writes=[f"vstat{P}"])
            op("scalar", lambda: EN["scalar"].activation(S["rstd"][:], S["vstat"][:], AF.Sqrt,
                                                         bias=epsb[:, 0:1]),
               reads=[f"vstat{P}", "epsb"], writes=[f"rstd{P}"])
            op("vector", lambda: EN["vector"].reciprocal(S["rstd"][:], S["rstd"][:]),
               reads=[f"rstd{P}"], writes=[f"rstd{P}"])
            op("vector", lambda: EN["vector"].tensor_scalar(
                ln[:], x0[:], S["mstat"][:, 0:1], S["rstd"][:, 0:1], ALU.subtract, ALU.mult),
                reads=[f"x0t{P}", f"mstat{P}", f"rstd{P}"], writes=[f"lnt{P}_scr"] + LNSL)

        def sa_transpose_ln(P, UB, ln, ttl, to_bf):
            for q in range(KT // 4):
                qb = UB + (q % 2) * 512
                kts = list(range(4 * q, 4 * q + 4))
                for j, kt in enumerate(kts):
                    op("tensor", lambda kt=kt, slot=qb + j * 128, ln=ln: EN["tensor"].transpose(
                        PS[:, slot:slot + 128], ln[:, kt * 128:(kt + 1) * 128], ident_f[:]),
                        reads=[f"lnt{P}_sl{kt}", "ident_f"], writes=psk(qb, qb + 512),
                        inc=(j == 3))
                op("scalar", lambda q=q, qb=qb, ln=ln: EN["scalar"].copy(
                    ln[:, q * 512:(q + 1) * 512], PS[:, qb:qb + 512]),
                    reads=psk(qb, qb + 512),
                    writes=[f"lnt{P}_sl{kt}" for kt in kts])
                if to_bf:
                    op("scalar", lambda q=q, ln=ln, ttl=ttl: EN["scalar"].copy(
                        ln1t[:].rearrange("p (k c) -> p k c", c=1024)[:, 4 * q:4 * q + 4, ttl * 128:(ttl + 1) * 128],
                        ln[:, q * 512:(q + 1) * 512].rearrange("p (k c) -> p k c", c=128)),
                        reads=[f"lnt{P}_sl{kt}" for kt in kts],
                        writes=[f"ln1t_{kt}" for kt in kts])

        def sa_seg0(g, tile):
            P = tile % 2
            op("sync", lambda tile=tile, P=P: EN["sync"].dma_start(
                x0t[P][:], u_h[tile * 128:(tile + 1) * 128, :]),
                writes=[f"x0t{P}"], inc="dx0")
            sa_layernorm(P, x0t[P], lnt[P], [f"lnt{P}_sl{k}" for k in range(KT)])

        def sa_seg1(g, tile):
            P = tile % 2
            UB = 3072 if P == 0 else 0
            S = sa[P]
            sa_transpose_ln(P, UB, lnt[P], tile % TT_G, False)
            for kt in range(KT):
                op("tensor", lambda kt=kt, P=P, UB=UB: EN["tensor"].matmul(
                    PS[0:R, UB + 512:UB + 640], wbr[:, kt * R:(kt + 1) * R],
                    lnt[P][:, kt * 128:(kt + 1) * 128],
                    start=(kt == 0), stop=(kt == KT - 1)),
                    reads=[f"lnt{P}_sl{kt}", "wbr"], writes=psk(UB + 512, UB + 640),
                    inc=(kt == KT - 1))
            op("vector", lambda S=S, UB=UB: EN["vector"].tensor_copy(
                S["s1b"][:, :], PS[0:R, UB + 512:UB + 640]),
                reads=psk(UB + 512, UB + 640), writes=[f"s1b{P}"])
            op("tensor", lambda S=S, UB=UB: EN["tensor"].matmul(
                PS[0:R, UB + 512:UB + 640], wmid[:, :], S["s1b"][:, :], start=True, stop=True),
                reads=["wmid", f"s1b{P}"], writes=psk(UB + 512, UB + 640), inc=True)
            op("vector", lambda S=S, UB=UB: EN["vector"].tensor_copy(
                S["s2b"][:, :], PS[0:R, UB + 512:UB + 640]),
                reads=psk(UB + 512, UB + 640), writes=[f"s2b{P}"])

        def sa_seg2(g, tile):
            P = tile % 2
            UB = 3072 if P == 0 else 0
            S = sa[P]
            for kt in range(KT):
                slot = UB + (kt % 2) * 512
                op("tensor", lambda kt=kt, slot=slot, S=S: EN["tensor"].matmul(
                    PS[:, slot:slot + 128], wcm[:, kt * 128:(kt + 1) * 128], S["s2b"][:, :],
                    start=True, stop=True),
                    reads=["wcm", f"s2b{P}"], writes=psk(slot, slot + 128), inc=True)
                op("vector", lambda kt=kt, slot=slot, P=P: EN["vector"].scalar_tensor_tensor(
                    lnt[P][:, kt * 128:(kt + 1) * 128], lnt[P][:, kt * 128:(kt + 1) * 128],
                    dvr[:, kt:kt + 1], PS[:, slot:slot + 128], ALU.mult, ALU.add),
                    reads=[f"lnt{P}_sl{kt}", "dvr"] + psk(slot, slot + 128),
                    writes=[f"lnt{P}_sl{kt}"])
            for half in range(2):
                for k8 in range(8):
                    kt = half * 8 + k8
                    slot = UB + k8 * 128
                    op("tensor", lambda kt=kt, slot=slot, P=P: EN["tensor"].transpose(
                        PS[:, slot:slot + 128], lnt[P][:, kt * 128:(kt + 1) * 128], ident_f[:]),
                        reads=[f"lnt{P}_sl{kt}", "ident_f"], writes=psk(slot, slot + 128),
                        inc=(k8 % 4 == 3))
                op("vector", lambda half=half, P=P, UB=UB: EN["vector"].tensor_add(
                    x0t[P][:, half * 1024:(half + 1) * 1024],
                    x0t[P][:, half * 1024:(half + 1) * 1024], PS[:, UB:UB + 1024]),
                    reads=[f"x0t{P}"] + psk(UB, UB + 1024), writes=[f"x0t{P}"])
            op("sync", lambda tile=tile, P=P: EN["sync"].dma_start(
                x1s_h[tile, :, :], x0t[P][:]),
                reads=[f"x0t{P}"], writes=[f"hbm_x1_{tile}"],
                inc="dx1a" if tile % 2 == 0 else "dx1b")
            if "x1" in taps:
                op("sync", lambda tile=tile, P=P: EN["sync"].dma_start(
                    tap_h["x1"][tile * 128:(tile + 1) * 128, :], x0t[P][:]),
                    reads=[f"x0t{P}"], writes=["ser_ta" if tile % 2 == 0 else "ser_tb"],
                    inc="dtapa" if tile % 2 == 0 else "dtapb")

        def sa_seg3(g, tile):
            P = tile % 2
            UB = 3072 if P == 0 else 0
            sa_layernorm(P, x0t[P], lnt[P], [f"lnt{P}_sl{k}" for k in range(KT)])
            sa_transpose_ln(P, UB, lnt[P], tile % TT_G, True)

        def sa_seg4(g, tile):
            P = tile % 2
            UB = 3072 if P == 0 else 0
            S = sa[P]
            ttl = tile % TT_G
            for kt in range(KT):
                op("tensor", lambda kt=kt, P=P, UB=UB: EN["tensor"].matmul(
                    PS[0:E, UB + 512:UB + 640], cwt[:, kt * E:(kt + 1) * E],
                    lnt[P][:, kt * 128:(kt + 1) * 128],
                    start=(kt == 0), stop=(kt == KT - 1)),
                    reads=[f"lnt{P}_sl{kt}", "cwt"], writes=psk(UB + 512, UB + 640),
                    inc=(kt == KT - 1))
            op("vector", lambda S=S, UB=UB: EN["vector"].tensor_copy(
                S["c8b"][:, :], PS[0:E, UB + 512:UB + 640]),
                reads=psk(UB + 512, UB + 640), writes=[f"c8b{P}"])
            op("tensor", lambda S=S, UB=UB: EN["tensor"].transpose(
                PS[:, UB + 512:UB + 520], S["c8b"][:, :], ident_f[0:E, 0:E]),
                reads=[f"c8b{P}", "ident_f"], writes=psk(UB + 512, UB + 520), inc=True)
            op("vector", lambda S=S, UB=UB: EN["vector"].tensor_copy(
                S["bidt"][:], PS[:, UB + 512:UB + 520]),
                reads=psk(UB + 512, UB + 520), writes=[f"bidt{P}"])
            op("vector", lambda S=S: EN["vector"].tensor_add(S["bidt"][:], S["bidt"][:], cbb[:, :]),
               reads=[f"bidt{P}", "cbb"], writes=[f"bidt{P}"])
            op("scalar", lambda S=S: EN["scalar"].activation(S["bidt"][:], S["bidt"][:], AF.Sigmoid),
               reads=[f"bidt{P}"], writes=[f"bidt{P}"])
            op("vector", lambda S=S: EN["vector"].tensor_mul(S["bidt"][:], S["bidt"][:], wlb[:, :]),
               reads=[f"bidt{P}", "wlb"], writes=[f"bidt{P}"])
            op("vector", lambda S=S: EN["vector"].max(S["t8"][:], S["bidt"][:]),
               reads=[f"bidt{P}"], writes=[f"t8{P}"])
            op("vector", lambda S=S: EN["vector"].tensor_single_scalar(
                S["pad1"][:], ones8[:], S["t8"][:, 0:1], ALU.mult),
                reads=["ones8", f"t8{P}"], writes=[f"pad1{P}"])
            op("vector", lambda S=S: EN["vector"].match_replace(
                S["rep"][:], S["pad1"][:, :], S["bidt"][:], -1e30),
                reads=[f"pad1{P}", f"bidt{P}"], writes=[f"rep{P}"])
            op("vector", lambda S=S: EN["vector"].tensor_single_scalar(
                S["eqA"][:], S["rep"][:], -1e29, ALU.is_lt),
                reads=[f"rep{P}"], writes=[f"eqA{P}"])
            op("vector", lambda S=S: EN["vector"].tensor_single_scalar(
                S["pad2"][:], ones8[:], S["t8"][:, 0:1], ALU.mult),
                reads=["ones8", f"t8{P}"], writes=[f"pad2{P}"])
            op("vector", lambda S=S: EN["vector"].tensor_copy(S["pad2"][:, 1:2], S["t8"][:, 1:2]),
               reads=[f"t8{P}"], writes=[f"pad2{P}"])
            op("vector", lambda S=S: EN["vector"].match_replace(
                S["rep"][:], S["pad2"][:, :], S["bidt"][:], -1e30),
                reads=[f"pad2{P}", f"bidt{P}"], writes=[f"rep{P}"])
            op("vector", lambda S=S: EN["vector"].tensor_single_scalar(
                S["eqB"][:], S["rep"][:], -1e29, ALU.is_lt),
                reads=[f"rep{P}"], writes=[f"eqB{P}"])
            op("vector", lambda S=S: EN["vector"].tensor_sub(S["d12"][:], S["t8"][:, 0:1], S["t8"][:, 1:2]),
               reads=[f"t8{P}"], writes=[f"d12{P}"])
            op("scalar", lambda S=S: EN["scalar"].activation(S["r1"][:, 0:1], S["d12"][:, 0:1], AF.Sigmoid),
               reads=[f"d12{P}"], writes=[f"r1{P}"])
            op("scalar", lambda S=S: EN["scalar"].activation(
                S["r2"][:, 0:1], S["r1"][:, 0:1], AF.Copy, scale=-1.0, bias=1.0),
                reads=[f"r1{P}"], writes=[f"r2{P}"])
            op("vector", lambda S=S: EN["vector"].scalar_tensor_tensor(
                S["wal"][:], S["eqA"][:], S["r1"][:, 0:1], S["eqA"][:], ALU.mult, ALU.bypass),
                reads=[f"eqA{P}", f"r1{P}"], writes=[f"wal{P}"])
            op("vector", lambda S=S: EN["vector"].tensor_sub(S["eqB"][:], S["eqB"][:], S["eqA"][:]),
               reads=[f"eqB{P}", f"eqA{P}"], writes=[f"eqB{P}"])
            op("vector", lambda S=S: EN["vector"].scalar_tensor_tensor(
                S["wal"][:], S["eqB"][:], S["r2"][:, 0:1], S["wal"][:], ALU.mult, ALU.add),
                reads=[f"eqB{P}", f"r2{P}", f"wal{P}"], writes=[f"wal{P}"])
            if "weall" in taps:
                op("sync", lambda tile=tile, S=S: EN["sync"].dma_start(
                    tap_h["weall"][tile * 128:(tile + 1) * 128, :], S["wal"][:]),
                    reads=[f"wal{P}"], writes=["ser_ta" if tile % 2 == 0 else "ser_tb"],
                    inc="dtapa" if tile % 2 == 0 else "dtapb")
            op("tensor", lambda S=S, UB=UB: EN["tensor"].transpose(
                PS[0:E, UB + 512:UB + 640], S["wal"][:], ident_f[:]),
                reads=[f"wal{P}", "ident_f"], writes=psk(UB + 512, UB + 640), inc=True)
            op("vector", lambda S=S, UB=UB: EN["vector"].tensor_copy(
                S["wet"][:, :], PS[0:E, UB + 512:UB + 640]),
                reads=psk(UB + 512, UB + 640), writes=[f"wet{P}"])
            for eh in range(2):
                for j in range(4):
                    e = eh * 4 + j
                    c0 = UB + eh * 512 + j * 128
                    op("tensor", lambda e=e, c0=c0, S=S: EN["tensor"].matmul(
                        PS[:, c0:c0 + 128], ohc[:, e * 128:(e + 1) * 128], S["wet"][:, :],
                        start=True, stop=True),
                        reads=["ohc", f"wet{P}"], writes=psk(c0, c0 + 128),
                        inc=(j == 3))
                op("vector", lambda eh=eh, ttl=ttl, UB=UB: EN["vector"].tensor_copy(
                    webc[:].rearrange("p (e c) -> p e c", c=1024)
                        [:, eh * 4:eh * 4 + 4, ttl * 128:(ttl + 1) * 128],
                    PS[:, UB + eh * 512: UB + eh * 512 + 512]
                        .rearrange("p (e c) -> p e c", c=128)),
                    reads=psk(UB + eh * 512, UB + eh * 512 + 512),
                    writes=[f"webc_{eh * 4 + j}" for j in range(4)])

        def emit_stage_a_group(g):
            segs = [sa_seg0, sa_seg1, sa_seg2, sa_seg3, sa_seg4]
            tiles = [g * TT_G + i for i in range(TT_G)]
            for r in range(2 * len(tiles) + len(segs)):
                for idx, t in enumerate(tiles):
                    j = r - 2 * idx
                    if 0 <= j < len(segs):
                        segs[j](g, t)

        def emit_xga(g):
            for e in range(E):
                buf, bk = wbuf[e % 3], f"wbuf{e % 3}"
                op("gpsimd", lambda e=e, buf=buf: EN["gpsimd"].dma_start(
                    buf[:, 0:KT * 128], xau_h[e]), writes=[bk], inc="dwb%d" % (e % 3))
                for kt in range(KT):
                    for fc in range(2):
                        c0 = 2048 + fc * 512
                        op("tensor", lambda kt=kt, fc=fc, buf=buf, c0=c0: EN["tensor"].matmul(
                            PS[:, c0:c0 + 512], buf[:, kt * 128:(kt + 1) * 128],
                            ln1t[:, kt * 1024 + fc * 512: kt * 1024 + (fc + 1) * 512],
                            start=(kt == 0), stop=(kt == KT - 1)),
                            reads=[bk, f"ln1t_{kt}"], writes=psk(c0, c0 + 512),
                            inc=(kt == KT - 1))
                sg_, su_ = e, 8 + e
                sgn, sun = (e + 1) % 8, 8 + (e + 1) % 8
                op("vector", lambda sg_=sg_: EN["vector"].tensor_single_scalar(
                    xga[0:64, sg_ * 1024:(sg_ + 1) * 1024], PS[0:64, 2048:3072],
                    SCALING, ALU.mult),
                    reads=psk(2048, 3072), writes=[f"xga_{sg_}a"])
                op("vector", lambda su_=su_: EN["vector"].tensor_single_scalar(
                    xga[0:64, su_ * 1024:(su_ + 1) * 1024], PS[64:128, 2048:3072],
                    SCALING, ALU.mult),
                    reads=psk(2048, 3072), writes=[f"xga_{su_}a"])
                if e < E - 1:
                    op("vector", lambda sgn=sgn: EN["vector"].tensor_single_scalar(
                        xga[64:128, sgn * 1024:(sgn + 1) * 1024], PS[0:64, 2048:3072],
                        -SCALING, ALU.mult),
                        reads=psk(2048, 3072), writes=[f"xga_{sgn}b"])
                    op("vector", lambda sun=sun: EN["vector"].tensor_single_scalar(
                        xga[64:128, sun * 1024:(sun + 1) * 1024], PS[64:128, 2048:3072],
                        -SCALING, ALU.mult),
                        reads=psk(2048, 3072), writes=[f"xga_{sun}b"])
            op("vector", lambda: EN["vector"].memset(xga[64:128, 0:1024], 0.0),
               writes=["xga_0b"])
            op("vector", lambda: EN["vector"].memset(xga[64:128, 8 * 1024:9 * 1024], 0.0),
               writes=["xga_8b"])

        def p1_regions(it):
            if it % 2 == 0:
                return 0, 1024
            return 2048, 3072

        def emit_p1_dma(g, it):
            buf, bk = wbuf[it % 3], f"wbuf{it % 3}"
            op("gpsimd", lambda it=it, buf=buf: EN["gpsimd"].dma_start(
                buf[:], wslab_h[it]), writes=[bk], inc="dwb%d" % (it % 3))

        def emit_p1_base_chunk(g, it, which, kt):
            buf, bk = wbuf[it % 3], f"wbuf{it % 3}"
            G0, U0 = p1_regions(it)
            base, col = (WC_BG, G0) if which == "g" else (WC_BU, U0)
            for fc in range(2):
                op("tensor", lambda kt=kt, fc=fc, buf=buf, base=base, col=col:
                    EN["tensor"].matmul(
                        PS[:, col + fc * 512: col + (fc + 1) * 512],
                        buf[:, base + kt * 128: base + (kt + 1) * 128],
                        ln1t[:, kt * 1024 + fc * 512: kt * 1024 + (fc + 1) * 512],
                        start=(kt == 0), stop=False, skip_group_check=True),
                    reads=[bk, f"ln1t_{kt}"], writes=[f"B{(col + fc * 512) // 512}"],
                    inc=False)

        def emit_p1_expert(g, it, e):
            buf, bk = wbuf[it % 3], f"wbuf{it % 3}"
            G0, U0 = p1_regions(it)
            if e == 0:
                op("scalar", lambda buf=buf, it=it: EN["scalar"].copy(
                    dabuf[it % 2][:], buf[:, WC_DA:WC_DA + E * R]),
                    reads=[bk], writes=[f"dabuf{it % 2}"])
            sg_, su_ = e, 8 + e
            for fc in range(2):
                op("tensor", lambda e=e, fc=fc, buf=buf, sg_=sg_, G0=G0:
                    EN["tensor"].matmul(
                        PS[:, G0 + fc * 512: G0 + (fc + 1) * 512],
                        buf[:, WC_GB + e * 128: WC_GB + (e + 1) * 128],
                        xga[:, sg_ * 1024 + fc * 512: sg_ * 1024 + (fc + 1) * 512],
                        start=False, stop=(e == E - 1), skip_group_check=True),
                    reads=[bk, f"xga_{sg_}a", f"xga_{sg_}b"],
                    writes=[f"B{(G0 + fc * 512) // 512}"], inc=True)
                op("tensor", lambda e=e, fc=fc, buf=buf, su_=su_, U0=U0:
                    EN["tensor"].matmul(
                        PS[:, U0 + fc * 512: U0 + (fc + 1) * 512],
                        buf[:, WC_UB + e * 128: WC_UB + (e + 1) * 128],
                        xga[:, su_ * 1024 + fc * 512: su_ * 1024 + (fc + 1) * 512],
                        start=False, stop=(e == E - 1), skip_group_check=True),
                    reads=[bk, f"xga_{su_}a", f"xga_{su_}b"],
                    writes=[f"B{(U0 + fc * 512) // 512}"], inc=True)
            sg2, ub2 = sgb[e % 2], ubb[e % 2]
            sgk, ubk = f"sgb{e % 2}", f"ubb{e % 2}"
            op("scalar", lambda sg2=sg2, G0=G0: EN["scalar"].activation(
                sg2[:], PS[:, G0:G0 + 1024], AF.Silu),
                reads=psk(G0, G0 + 1024), writes=[sgk])
            op("scalar", lambda ub2=ub2, U0=U0: EN["scalar"].copy(
                ub2[:], PS[:, U0:U0 + 1024]),
                reads=psk(U0, U0 + 1024), writes=[ubk])
            op("vector", lambda e=e, sg2=sg2, ub2=ub2: EN["vector"].tensor_mul(
                h8[:, e * 1024:(e + 1) * 1024], sg2[:], ub2[:]),
                reads=[sgk, ubk], writes=[f"h8_{e}"])
            hwv, hwk = hmixw[it % 2], f"hmixw{it % 2}"
            if e == 0:
                op("vector", lambda hwv=hwv: EN["vector"].tensor_mul(
                    hwv[:], h8[:, 0:1024], webc[:, 0:1024]),
                    reads=["h8_0", "webc_0"], writes=[hwk])
            else:
                op("vector", lambda e=e, sg2=sg2: EN["vector"].tensor_mul(
                    sg2[:], h8[:, e * 1024:(e + 1) * 1024],
                    webc[:, e * 1024:(e + 1) * 1024]),
                    reads=[f"h8_{e}", f"webc_{e}"], writes=[sgk])
                op("vector", lambda hwv=hwv, sg2=sg2: EN["vector"].tensor_add(
                    hwv[:], hwv[:], sg2[:]), reads=[hwk, sgk], writes=[hwk])

        def emit_p1_hmix_store(g, it):
            hwv, hwk = hmixw[it % 2], f"hmixw{it % 2}"
            op("sync", lambda it=it, g=g, hwv=hwv: EN["sync"].dma_start(
                hx_h[it, :, g * TG:(g + 1) * TG], hwv[:]),
                reads=[hwk], writes=[f"hbm_hx_{it}_{g}"], inc="dhx%d" % (it % 2))
            if "hmix" in taps:
                op("sync", lambda it=it, g=g, hwv=hwv: EN["sync"].dma_start(
                    tap_h["hmix"][it, :, g * TG:(g + 1) * TG], hwv[:]),
                    reads=[hwk], writes=["ser_ta" if it % 2 == 0 else "ser_tb"],
                    inc="dtapa" if it % 2 == 0 else "dtapb")

        def emit_p1_dA_pair(g, it, pr):
            dab, dak = dabuf[it % 2], f"dabuf{it % 2}"
            G0r, U0r = p1_regions(it)
            G0 = G0r if pr < 2 else U0r
            for hilo in range(2):
                e = 2 * pr + hilo
                for fc in range(2):
                    op("tensor", lambda e=e, fc=fc, hilo=hilo, dab=dab, G0=G0:
                        EN["tensor"].matmul(
                            PS[hilo * 64:(hilo + 1) * 64, G0 + fc * 512: G0 + (fc + 1) * 512],
                            dab[:, e * 64:(e + 1) * 64],
                            h8[:, e * 1024 + fc * 512: e * 1024 + (fc + 1) * 512],
                            start=True, stop=True),
                        reads=[dak, f"h8_{e}"],
                        writes=[f"B{(G0 + fc * 512) // 512}"], inc=True)
            for fc in range(2):
                c0 = G0 + fc * 512
                hs0 = pr * 1024 + fc * 512
                if it == 0:
                    op("vector", lambda c0=c0, hs0=hs0: EN["vector"].tensor_single_scalar(
                        hab[:, hs0:hs0 + 512], PS[:, c0:c0 + 512], SCALING, ALU.mult),
                        reads=psk(c0, c0 + 512), writes=[f"hab_{pr}_{fc}"])
                else:
                    op("vector", lambda c0=c0, hs0=hs0: EN["vector"].scalar_tensor_tensor(
                        hab[:, hs0:hs0 + 512], PS[:, c0:c0 + 512], SCALING,
                        hab[:, hs0:hs0 + 512], ALU.mult, ALU.add),
                        reads=psk(c0, c0 + 512) + [f"hab_{pr}_{fc}"],
                        writes=[f"hab_{pr}_{fc}"])

        def emit_phase1_group(g):
            emit_p1_dma(g, 0)
            emit_p1_dma(g, 1)
            for kt in range(KT):
                emit_p1_base_chunk(g, 0, "u", kt)
            for kt in range(KT):
                emit_p1_base_chunk(g, 0, "g", kt)
            for it in range(NIT):
                if it + 2 < NIT:
                    emit_p1_dma(g, it + 2)
                for e in range(E):
                    if it > 0 and e < 4:
                        emit_p1_dA_pair(g, it - 1, e)
                    if it + 1 < NIT:
                        if e in (2, 3):
                            for kt in range(8 * (e - 2), 8 * (e - 2) + 8):
                                emit_p1_base_chunk(g, it + 1, "g", kt)
                        elif e >= 4:
                            for kt in range(4 * (e - 4), 4 * (e - 4) + 4):
                                emit_p1_base_chunk(g, it + 1, "u", kt)
                    emit_p1_expert(g, it, e)
                emit_p1_hmix_store(g, it)
            for pr in range(4):
                emit_p1_dA_pair(g, NIT - 1, pr)

        def emit_haw(g):
            for pr in range(4):
                for hilo in range(2):
                    e = 2 * pr + hilo
                    op("vector", lambda pr=pr, hilo=hilo, e=e: EN["vector"].tensor_mul(
                        h8[hilo * 64:(hilo + 1) * 64, pr * 1024:(pr + 1) * 1024],
                        hab[hilo * 64:(hilo + 1) * 64, pr * 1024:(pr + 1) * 1024],
                        webc[hilo * 64:(hilo + 1) * 64, e * 1024:(e + 1) * 1024]),
                        reads=[f"hab_{pr}_0", f"hab_{pr}_1", f"webc_{e}"],
                        writes=[f"hawstg_{pr}", f"h8_{pr}"])
            for pr in range(4):
                op("sync", lambda pr=pr, g=g: EN["sync"].dma_start(
                    haw_h[:, pr, g, :], h8[:, pr * 1024:(pr + 1) * 1024]),
                    reads=[f"hawstg_{pr}", f"h8_{pr}"],
                    writes=[f"hbm_haw_{g}"],
                    inc="dhw0" if g == 0 else "dhw1")

        def emit_phase2():
            GA = 7
            NG = NIT // GA
            op("sync", lambda: EN["sync"].dma_start(
                hawb[:], haw_h.rearrange("p a b c -> p (a b c)")),
                reads=["hbm_haw_0", "hbm_haw_1"], writes=["hawball"], inc="dhg0")
            for dq in range(4):
                for gi in range(NG):
                    buf, bk = wbuf[gi % 2], f"wbuf{gi % 2}"
                    hg, hk = hmg[gi % 2], f"hmg{gi % 2}"
                    op("gpsimd", lambda dq=dq, gi=gi, buf=buf: EN["gpsimd"].dma_start(
                        buf[:, 0:GA * 512].rearrange("p (g c) -> p g c", g=GA),
                        bd_h[dq, gi * GA:(gi + 1) * GA].rearrange("g p c -> p g c")),
                        writes=[bk + "bd"], inc="dwb%d" % (gi % 2))
                    for j in range(GA):
                        it = gi * GA + j
                        rd = [f"hbm_hx_{it}_{gg}" for gg in range(NGRP)]
                        op("sync", lambda j=j, it=it, hg=hg: EN["sync"].dma_start(
                            hg[:, j * 2048:(j + 1) * 2048], hx_h[it, :, :]),
                            reads=rd, writes=[hk + f"x{j}"], inc="dhg%d" % (gi % 2))
                    tk.mark_done([hk + f"x{j}" for j in range(GA)], "dhg%d" % (gi % 2))
                    for dtl in range(4):
                        pc = 0 if dtl % 2 == 0 else 2048
                        for j in range(GA):
                            for fc in range(4):
                                op("tensor", lambda dtl=dtl, j=j, fc=fc, buf=buf, hg=hg, pc=pc:
                                    EN["tensor"].matmul(
                                        PS[:, pc + fc * 512: pc + (fc + 1) * 512],
                                        buf[:, j * 512 + dtl * 128: j * 512 + (dtl + 1) * 128],
                                        hg[:, j * 2048 + fc * 512: j * 2048 + (fc + 1) * 512],
                                        start=(j == 0), stop=(j == GA - 1)),
                                    reads=[bk + "bd", hk + f"x{j}"],
                                    writes=[f"B{(pc + fc * 512) // 512}"],
                                    inc=(j == GA - 1))
                        if gi == 0:
                            op("vector", lambda dtl=dtl, pc=pc: EN["vector"].tensor_copy(
                                outacc[:, dtl * 2048:(dtl + 1) * 2048], PS[:, pc:pc + 2048]),
                                reads=psk(pc, pc + 2048), writes=[f"oa_{dtl}"])
                        else:
                            op("vector", lambda dtl=dtl, pc=pc: EN["vector"].tensor_add(
                                outacc[:, dtl * 2048:(dtl + 1) * 2048],
                                outacc[:, dtl * 2048:(dtl + 1) * 2048], PS[:, pc:pc + 2048]),
                                reads=psk(pc, pc + 2048) + [f"oa_{dtl}"], writes=[f"oa_{dtl}"])
                op("gpsimd", lambda dq=dq: EN["gpsimd"].dma_start(
                    wbuf[0][:, 0:4 * 512].rearrange("p (q c) -> p q c", q=4),
                    dbcat_h[:, :, dq * 512:(dq + 1) * 512].rearrange("q p c -> p q c")),
                    writes=["wbuf0bd"], inc="dwb0")
                for dtl in range(4):
                    pc = 0 if dtl % 2 == 0 else 2048
                    for pr in range(4):
                        for fc in range(4):
                            hkey = ["hawball"]
                            op("tensor", lambda dtl=dtl, pr=pr, fc=fc, pc=pc:
                                EN["tensor"].matmul(
                                    PS[:, pc + fc * 512: pc + (fc + 1) * 512],
                                    wbuf[0][:, pr * 512 + dtl * 128: pr * 512 + (dtl + 1) * 128],
                                    hawb[:, pr * TC + fc * 512: pr * TC + (fc + 1) * 512],
                                    start=(pr == 0), stop=(pr == 3)),
                                reads=["wbuf0bd"] + hkey,
                                writes=[f"B{(pc + fc * 512) // 512}"],
                                inc=(pr == 3))
                    op("vector", lambda dtl=dtl, pc=pc: EN["vector"].tensor_add(
                        outacc[:, dtl * 2048:(dtl + 1) * 2048],
                        outacc[:, dtl * 2048:(dtl + 1) * 2048], PS[:, pc:pc + 2048]),
                        reads=psk(pc, pc + 2048) + [f"oa_{dtl}"], writes=[f"oa_{dtl}"])
                def x1load(tt, dq=dq):
                    xb, xk = x1buf[tt % 4], f"x1buf{tt % 4}"
                    op("sync", lambda tt=tt, dq=dq, xb=xb: EN["sync"].dma_start(
                        xb[:], x1s_h[tt, :, dq * 512:(dq + 1) * 512]),
                        reads=[f"hbm_x1_{tt}"], writes=[xk], inc="dxb%d" % (tt % 4))
                x1load(0)
                x1load(1)
                for tt in range(NTT):
                    xb, xk = x1buf[tt % 4], f"x1buf{tt % 4}"
                    orow, ork = outrow[tt % 4], f"outrow{tt % 4}"
                    if tt + 2 < NTT:
                        x1load(tt + 2)
                    for dtl in range(4):
                        slot = 3072 + dtl * 128
                        op("tensor", lambda dtl=dtl, tt=tt, slot=slot: EN["tensor"].transpose(
                            PS[:, slot:slot + 128],
                            outacc[:, dtl * 2048 + tt * 128: dtl * 2048 + (tt + 1) * 128],
                            ident_f[:]),
                            reads=[f"oa_{dtl}", "ident_f"], writes=psk(slot, slot + 128),
                            inc=(dtl == 3))
                    op("vector", lambda xb=xb, orow=orow: EN["vector"].tensor_add(
                        orow[:], PS[:, 3072:3584], xb[:]),
                        reads=psk(3072, 3584) + [xk], writes=[ork])
                    op("sync", lambda tt=tt, dq=dq, orow=orow: EN["sync"].dma_start(
                        y_h[tt * 128:(tt + 1) * 128, dq * 512:(dq + 1) * 512], orow[:]),
                        reads=[ork], writes=["hbm_y"], inc="dya" if tt % 2 == 0 else "dyb")

        def build_all():
            stage = [0]

            def go():
                stage[0] += 1
                return stop_after is None or stage[0] <= stop_after
            emit_init()
            done = False
            for g in range(NGRP):
                if g > 0:
                    tk.barrier()
                if go():
                    emit_stage_a_group(g)
                else:
                    done = True
                if not done and go():
                    emit_xga(g)
                else:
                    done = True
                tk.barrier()
                if not done and go():
                    emit_phase1_group(g)
                else:
                    done = True
                if not done and go():
                    emit_haw(g)
                else:
                    done = True
            tk.barrier()
            if not done and go():
                emit_phase2()
            tk.barrier()

        for engname in ENGS:
            def body(engobj, engname=engname):
                EN.clear()
                EN[engname] = engobj
                tk.reset(engname, engobj)
                build_all()
            getattr(blk, engname)(body)

    return nc, sorted(taps)



def _prep_inputs(L0_attn_u, L0_attn_A_factors, L0_attn_W_b_rank, L0_attn_W_b_state,
                 L0_attn_W_c_rank, L0_attn_W_c_model, L0_attn_D_vec,
                 L0_ffn_expert_wealth, L0_ffn_conf_W, L0_ffn_conf_b,
                 L0_ffn_base_gate, L0_ffn_base_up, L0_ffn_base_down,
                 L0_ffn_gate_A, L0_ffn_gate_B, L0_ffn_up_A, L0_ffn_up_B,
                 L0_ffn_down_A, L0_ffn_down_B):
    f = lambda a: np.ascontiguousarray(np.asarray(a), dtype=np.float32)
    u = f(L0_attn_u).reshape(NCORES * TC, D)
    bg = f(L0_ffn_base_gate).reshape(KT, 128, NIT, 128).transpose(2, 1, 0, 3).reshape(NIT, 128, KT * 128)
    bu = f(L0_ffn_base_up).reshape(KT, 128, NIT, 128).transpose(2, 1, 0, 3).reshape(NIT, 128, KT * 128)

    def gbcat(W):
        Wt = f(W).reshape(E, R, NIT, 128)
        cur = Wt.transpose(2, 1, 0, 3)
        prv = Wt[(np.arange(E) - 1) % E].transpose(2, 1, 0, 3)
        return np.concatenate([cur, prv], axis=1).reshape(NIT, 128, E * 128)

    dA = f(L0_ffn_down_A).reshape(E, NIT, 128, R).transpose(1, 2, 0, 3).reshape(NIT, 128, E * R)
    wslab = np.ascontiguousarray(
        np.concatenate([bg, bu, gbcat(L0_ffn_gate_B), gbcat(L0_ffn_up_B), dA], axis=2))
    gA = f(L0_ffn_gate_A).reshape(E, KT, 128, R)
    uA = f(L0_ffn_up_A).reshape(E, KT, 128, R)
    xau = np.concatenate([gA, uA], axis=3)
    xau = np.ascontiguousarray(xau.transpose(0, 2, 1, 3).reshape(E, 128, KT * 128))
    bd = np.ascontiguousarray(
        f(L0_ffn_base_down).reshape(NIT, 128, 4, 512).transpose(2, 0, 1, 3))
    dbcat = np.ascontiguousarray(f(L0_ffn_down_B).reshape(4, 2 * R, D))
    return {
        "u": u,
        "wslab": wslab,
        "xau": xau,
        "bdslab": bd,
        "dbcat": dbcat,
        "wbr": f(L0_attn_W_b_rank),
        "wbs": f(L0_attn_W_b_state),
        "wcr": f(L0_attn_W_c_rank),
        "wcm": f(L0_attn_W_c_model),
        "afac": f(L0_attn_A_factors).reshape(1, 32),
        "dvecr": np.ascontiguousarray(f(L0_attn_D_vec).reshape(KT, 128).T),
        "confwt": np.ascontiguousarray(f(L0_ffn_conf_W).T),
        "confb": f(L0_ffn_conf_b).reshape(1, E),
        "wealth": f(L0_ffn_expert_wealth).reshape(1, E),
    }


def kernel(**inputs):
    ins = _prep_inputs(**inputs)
    nc, _ = build_core_program()
    u = ins.pop("u")
    in_maps = []
    for c in range(NCORES):
        m = dict(ins)
        m["u"] = np.ascontiguousarray(u[c * TC:(c + 1) * TC])
        in_maps.append(m)
    res = run_bass_kernel_spmd(nc, in_maps, core_ids=list(range(NCORES)))
    out = np.concatenate([r["y"] for r in res.results], axis=0)
    return out.reshape(2, 8192, 2048)


# version 17
# speedup vs baseline: 1.1243x; 1.0408x over previous
import numpy as np
from contextlib import ExitStack

import concourse.bass as bass
import concourse.mybir as mybir
from concourse.bass_utils import run_bass_kernel_spmd

F32 = mybir.dt.float32
BF16 = mybir.dt.bfloat16
AF = mybir.ActivationFunctionType
ALU = mybir.AluOpType
AX = mybir.AxisListType

NCORES = 8
TC = 2048
NGRP = 2
TG = TC // NGRP
NTT = TC // 128
TT_G = TG // 128
D = 2048
KT = D // 128
I = 7168
NIT = I // 128
E = 8
R = 64
NST = 256
LEV = 8
EPS = 1e-5
SCALING = 16.0 / 64.0

WC_BG = 0
WC_BU = WC_BG + KT * 128
WC_GB = WC_BU + KT * 128
WC_UB = WC_GB + E * 128
WC_DA = WC_UB + E * 128
WCOLS = WC_DA + E * 64

ENGS = ("gpsimd", "scalar", "vector", "sync", "tensor")
SA_LIMIT = 99
DSEMS = ("dhw0", "dhw1", "dwb0", "dwb1", "dwb2", "dhg0", "dhg1", "dx0", "dx1a", "dx1b", "dhx0",
         "dhx1", "dxb0", "dxb1", "dxb2", "dxb3", "dya", "dyb", "dini", "dtapa", "dtapb")


class Trk:

    def __init__(self, nc, sems):
        self.nc = nc
        self.sem = sems

    def reset(self, current, engobj):
        self.current = current
        self.engobj = engobj
        self.cnt = {e: 0 for e in ENGS}
        for s in DSEMS:
            self.cnt[s] = 0
        self.seq = {e: 0 for e in ENGS}
        self.incseqs = {e: [] for e in ENGS}
        self.w = {}
        self.r = {}
        self.waited = {}

    def _wait(self, semname, count):
        if count <= self.waited.get(semname, 0):
            return
        self.waited[semname] = count
        self.engobj.wait_ge(self.sem[semname], count)

    def _resolve(self, k, tok):
        semname, count, seq = tok
        if count is not None:
            return semname, count
        import bisect
        lst = self.incseqs[semname]
        idx = bisect.bisect_left(lst, seq)
        if idx >= len(lst):
            raise AssertionError(
                f"dep on un-inc'd {semname} op (key={k}) with no later inc yet")
        return semname, idx + 1

    def op(self, eng, fn, reads=(), writes=(), inc=True):
        deps = []
        for k in reads:
            if k in self.w:
                deps.append((k, self.w[k]))
        for k in writes:
            if k in self.w:
                deps.append((k, self.w[k]))
            for tok in self.r.get(k, {}).values():
                deps.append((k, tok))
        need = {}
        for k, tok in deps:
            if tok[0] == eng == "tensor":
                continue
            semname, count = self._resolve(k, tok)
            need[semname] = max(need.get(semname, 0), count)
        dma = isinstance(inc, str)
        if eng == self.current:
            for semname in sorted(need):
                self._wait(semname, need[semname])
            ins = fn()
        else:
            ins = None
        myseq = self.seq[eng]
        self.seq[eng] += 1
        if dma:
            semname = inc
            self.cnt[semname] += 16
            if ins is not None:
                ins.then_inc(self.sem[semname], 16)
            tok = (semname, self.cnt[semname], myseq)
        elif inc:
            self.cnt[eng] += 1
            self.incseqs[eng].append(myseq)
            if ins is not None:
                ins.then_inc(self.sem[eng], 1)
            tok = (eng, self.cnt[eng], myseq)
        else:
            tok = (eng, None, myseq)
        for k in writes:
            self.w[k] = tok
            self.r[k] = {}
        for k in reads:
            self.r.setdefault(k, {})[tok[0]] = tok
        return ins

    def barrier(self):
        for semname in list(ENGS) + list(DSEMS):
            c = self.cnt[semname]
            if c > 0 and semname != self.current:
                self._wait(semname, c)

    def mark_done(self, keys, semname):
        tok = (semname, self.cnt[semname], 0)
        for k in keys:
            if k in self.w:
                self.w[k] = tok


def psk(c0, c1):
    return [f"B{b}" for b in range(c0 // 512, (c1 + 511) // 512)]


def build_core_program(taps=(), stop_after=None):
    nc = bass.Bass("TRN2", target_bir_lowering=False, debug=False)
    taps = set(taps)

    u_h = nc.dram_tensor("u", [TC, D], F32, kind="ExternalInput").ap()
    wslab_h = nc.dram_tensor("wslab", [NIT, 128, WCOLS], F32, kind="ExternalInput").ap()
    xau_h = nc.dram_tensor("xau", [E, 128, KT * 128], F32, kind="ExternalInput").ap()
    bd_h = nc.dram_tensor("bdslab", [4, NIT, 128, 512], F32, kind="ExternalInput").ap()
    dbcat_h = nc.dram_tensor("dbcat", [4, 128, D], F32, kind="ExternalInput").ap()
    wbr_h = nc.dram_tensor("wbr", [D, R], F32, kind="ExternalInput").ap()
    wbs_h = nc.dram_tensor("wbs", [R, NST], F32, kind="ExternalInput").ap()
    wcr_h = nc.dram_tensor("wcr", [NST, R], F32, kind="ExternalInput").ap()
    wcm_h = nc.dram_tensor("wcm", [R, D], F32, kind="ExternalInput").ap()
    afac_h = nc.dram_tensor("afac", [1, 32], F32, kind="ExternalInput").ap()
    dvec_h = nc.dram_tensor("dvecr", [128, KT], F32, kind="ExternalInput").ap()
    cwt_h = nc.dram_tensor("confwt", [D, E], F32, kind="ExternalInput").ap()
    cb_h = nc.dram_tensor("confb", [1, E], F32, kind="ExternalInput").ap()
    wl_h = nc.dram_tensor("wealth", [1, E], F32, kind="ExternalInput").ap()
    y_h = nc.dram_tensor("y", [TC, D], F32, kind="ExternalOutput").ap()
    x1s_h = nc.dram_tensor("x1scr", [NTT, 128, D], F32, kind="Internal").ap()
    hx_h = nc.dram_tensor("hmixscr", [NIT, 128, TC], BF16, kind="Internal").ap()
    haw_h = nc.dram_tensor("hawscr", [128, 4, NGRP, TG], BF16, kind="Internal").ap()
    tap_h = {}
    if "x1" in taps:
        tap_h["x1"] = nc.dram_tensor("tap_x1", [TC, D], F32, kind="ExternalOutput").ap()
    if "weall" in taps:
        tap_h["weall"] = nc.dram_tensor("tap_weall", [TC, E], F32, kind="ExternalOutput").ap()
    if "hmix" in taps:
        tap_h["hmix"] = nc.dram_tensor("tap_hmix", [NIT, 128, TC], BF16, kind="ExternalOutput").ap()
    if "lnt" in taps:
        tap_h["lnt"] = nc.dram_tensor("tap_lnt", [NTT, 128, D], F32, kind="ExternalOutput").ap()

    es = ExitStack()
    sb = lambda n, shape, dt: es.enter_context(nc.sbuf_tensor(n, shape, dt))
    with es:
        sems = {}
        for e_ in ENGS:
            sems[e_] = es.enter_context(nc.semaphore(f"sem_{e_}"))
        for s_ in DSEMS:
            sems[s_] = es.enter_context(nc.semaphore(f"semd_{s_}"))
        tk = Trk(nc, sems)

        wbuf = [sb(f"wbuf{i}", [128, WCOLS], BF16) for i in range(3)]
        wbr = sb("wbr_s", [128, KT * R], F32)
        wcm = sb("wcm_s", [R, D], F32)
        wcr = sb("wcr_s", [128, 2 * R], F32)
        wbsA = sb("wbsA", [R, NST], F32)
        wbsB = sb("wbsB", [R, NST], F32)
        wbsT = sb("wbsT", [128, 2 * R], F32)
        wmid = sb("wmid", [R, R], F32)
        afb = sb("afb", [R, 32], F32)
        cwt = sb("cwt_s", [128, KT * E], F32)
        cbb = sb("cbb", [128, E], F32)
        wlb = sb("wlb", [128, E], F32)
        dvr = sb("dvr", [128, KT], F32)
        ident_f = sb("ident_f", [128, 128], F32)
        ones1 = sb("ones1", [1, 128], F32)
        ones8 = sb("ones8", [128, E], F32)
        ohc = sb("ohc", [E, E * 128], F32)
        epsb = sb("epsb", [128, 1], F32)
        ph1 = ExitStack()
        sb1 = lambda n, shape, dt: ph1.enter_context(nc.sbuf_tensor(n, shape, dt))
        ln1t = sb1("ln1t", [128, KT * 1024], BF16)
        xga = sb1("xga", [128, 16 * 1024], BF16)
        webc = sb1("webc", [128, E * 1024], BF16)
        hab = sb1("hab", [128, 4 * 1024], F32)
        saes = ExitStack()
        sbA = lambda n, shape, dt: saes.enter_context(nc.sbuf_tensor(n, shape, dt))
        x0t = [sbA(f"x0t{i}", [128, D], F32) for i in range(2)]
        lnt = [sbA(f"lnt{i}", [128, D], F32) for i in range(2)]
        sa = []
        for i in range(2):
            sa.append({nm: sbA(f"{nm}{i}", [128, E], F32)
                       for nm in ("t8", "pad1", "pad2", "rep", "eqA", "eqB", "wal", "bidt")})
            for nm in ("d12", "r1", "r2", "mstat", "vstat", "sstat", "rstd"):
                sa[i][nm] = sbA(f"{nm}{i}", [128, 1], F32)
            sa[i]["wet"] = sbA(f"wet{i}", [E, 128], F32)
            sa[i]["s1b"] = sbA(f"s1b{i}", [R, 128], F32)
            sa[i]["s2b"] = sbA(f"s2b{i}", [R, 128], F32)
            sa[i]["c8b"] = sbA(f"c8b{i}", [E, 128], F32)
        saes.close()
        hmixw = [sb1(f"hmixw{i}", [128, 1024], BF16) for i in range(2)]
        sgb = [sb1(f"sgb{i}", [128, 1024], BF16) for i in range(2)]
        ubb = [sb1(f"ubb{i}", [128, 1024], BF16) for i in range(2)]
        h8 = sb1("h8", [128, E * 1024], BF16)
        dabuf = [sb1(f"dabuf{i}", [128, E * R], BF16) for i in range(2)]
        ph1.close()
        outacc = sb("outacc", [128, 4 * 2048], F32)
        hawb = sb("hawb2", [128, 4 * TC], BF16)
        hmg = [sb(f"hmg{i}", [128, 7 * 2048], BF16) for i in range(2)]
        x1buf = [sb(f"x1buf{i}", [128, 512], F32) for i in range(4)]
        outrow = [sb(f"outrow{i}", [128, 512], F32) for i in range(4)]

        PS = es.enter_context(nc.psum_tensor([128, 4096], F32))

        blk = es.enter_context(nc.Block())
        EN = {}

        def op(eng, fn, reads=(), writes=(), inc=True):
            return tk.op(eng, fn, reads=reads, writes=writes, inc=inc)

        def emit_init():
            op("gpsimd", lambda: EN["gpsimd"].iota(
                sa[0]["mstat"][:], [[1, 1]], base=0, channel_multiplier=1,
                allow_small_or_imprecise_dtypes=True), writes=["mstat0"])
            op("gpsimd", lambda: EN["gpsimd"].iota(
                lnt[0][:, 0:128], [[1, 128]], base=0, channel_multiplier=0,
                allow_small_or_imprecise_dtypes=True), writes=["lnt0_sl0"])
            op("vector", lambda: EN["vector"].tensor_single_scalar(
                ident_f[:], lnt[0][:, 0:128], sa[0]["mstat"][:, 0:1], ALU.is_equal),
                reads=["lnt0_sl0", "mstat0"], writes=["ident_f"])
            op("vector", lambda: EN["vector"].memset(ones1[:], 1.0), writes=["ones1"])
            op("vector", lambda: EN["vector"].memset(epsb[:], EPS), writes=["epsb"])
            op("vector", lambda: EN["vector"].memset(ones8[:], 1.0), writes=["ones8"])
            op("gpsimd", lambda: EN["gpsimd"].iota(
                ohc[:].rearrange("p (b c) -> p b c", b=E), [[1, E], [0, 128]],
                base=0, channel_multiplier=0,
                allow_small_or_imprecise_dtypes=True), writes=["ohc"])
            op("vector", lambda: EN["vector"].tensor_single_scalar(
                ohc[:], ohc[:], sa[0]["mstat"][0:E, 0:1], ALU.is_equal),
                reads=["ohc", "mstat0"], writes=["ohc"])
            op("sync", lambda: EN["sync"].dma_start(
                wbr[:].rearrange("p (k r) -> p k r", r=R),
                wbr_h.rearrange("(k p) r -> p k r", p=128)), writes=["wbr"], inc="dini")
            op("sync", lambda: EN["sync"].dma_start(wcm[:], wcm_h[:, :]), writes=["wcm"], inc="dini")
            op("sync", lambda: EN["sync"].dma_start(
                wcr[:].rearrange("p (k r) -> p k r", r=R),
                wcr_h.rearrange("(k p) r -> p k r", p=128)), writes=["wcr"], inc="dini")
            op("sync", lambda: EN["sync"].dma_start(wbsA[:], wbs_h[:, :]), writes=["wbsA"], inc="dini")
            op("sync", lambda: EN["sync"].dma_start(
                cwt[:].rearrange("p (k e) -> p k e", e=E),
                cwt_h.rearrange("(k p) e -> p k e", p=128)), writes=["cwt"], inc="dini")
            op("sync", lambda: EN["sync"].dma_start(afb[0:1, :], afac_h[:, :]), writes=["afb"], inc="dini")
            op("sync", lambda: EN["sync"].dma_start(cbb[0:1, :], cb_h[:, :]), writes=["cbb"], inc="dini")
            op("sync", lambda: EN["sync"].dma_start(wlb[0:1, :], wl_h[:, :]), writes=["wlb"], inc="dini")
            op("sync", lambda: EN["sync"].dma_start(dvr[:], dvec_h[:, :]), writes=["dvr"], inc="dini")
            tk.mark_done(["wbr", "wcm", "wcr", "wbsA", "cwt", "afb", "cbb", "wlb", "dvr"], "dini")
            op("tensor", lambda: EN["tensor"].matmul(
                PS[:, 3072:3104], ones1[:], afb[0:1, :], start=True, stop=True),
                reads=["ones1", "afb"], writes=psk(3072, 3104), inc=True)
            op("tensor", lambda: EN["tensor"].matmul(
                PS[:, 3104:3112], ones1[:], cbb[0:1, :], start=True, stop=True),
                reads=["ones1", "cbb"], writes=psk(3104, 3112), inc=True)
            op("tensor", lambda: EN["tensor"].matmul(
                PS[:, 3112:3120], ones1[:], wlb[0:1, :], start=True, stop=True),
                reads=["ones1", "wlb"], writes=psk(3112, 3120), inc=True)
            op("vector", lambda: EN["vector"].tensor_copy(afb[:, :], PS[0:R, 3072:3104]),
               reads=psk(3072, 3104), writes=["afb"])
            op("vector", lambda: EN["vector"].tensor_copy(cbb[:, :], PS[:, 3104:3112]),
               reads=psk(3104, 3112), writes=["cbb"])
            op("vector", lambda: EN["vector"].tensor_copy(wlb[:, :], PS[:, 3112:3120]),
               reads=psk(3112, 3120), writes=["wlb"])
            src, dst, srck, dstk = wbsA, wbsB, "wbsA", "wbsB"
            for lvl in range(LEV):
                w = 1 << (LEV - 1 - lvl)
                for j in range(2):
                    a0 = afb[0:R, (lvl * 4 + j):(lvl * 4 + j) + 1]
                    a1 = afb[0:R, (lvl * 4 + 2 + j):(lvl * 4 + 2 + j) + 1]
                    vsrc = src[:, 0:NST].rearrange("p (h two w) -> p h two w", two=2, w=w)
                    vdst = dst[:, 0:NST].rearrange("p (h two w) -> p h two w", two=2, w=w)
                    scr = lnt[0][0:R, 0:NST // 2].rearrange("p (h w) -> p h w", w=w)
                    op("vector", lambda vsrc=vsrc, a0=a0, scr=scr: EN["vector"].tensor_single_scalar(
                        scr, vsrc[:, :, 0, :], a0, ALU.mult),
                        reads=[srck, "afb"], writes=["lnt0_scr"] + [f"lnt0_sl{k}" for k in range(2)])
                    op("vector", lambda vsrc=vsrc, vdst=vdst, a1=a1, j=j, scr=scr:
                        EN["vector"].scalar_tensor_tensor(
                            vdst[:, :, j, :], vsrc[:, :, 1, :], a1, scr, ALU.mult, ALU.add),
                        reads=[srck, "afb", "lnt0_scr"], writes=[dstk])
                src, dst, srck, dstk = dst, src, dstk, srck
            for k in range(2):
                op("tensor", lambda k=k, src=src: EN["tensor"].transpose(
                    PS[:, 3072 + k * 512: 3072 + k * 512 + 64],
                    src[0:R, k * 128:(k + 1) * 128], ident_f[0:R, 0:R]),
                    reads=[srck, "ident_f"], writes=psk(3072 + k * 512, 3072 + k * 512 + 64),
                    inc=True)
            op("vector", lambda: EN["vector"].tensor_copy(wbsT[:, 0:R], PS[:, 3072:3136]),
               reads=psk(3072, 3136), writes=["wbsT"])
            op("vector", lambda: EN["vector"].tensor_copy(wbsT[:, R:2 * R], PS[:, 3584:3648]),
               reads=psk(3584, 3648), writes=["wbsT"])
            for k in range(2):
                op("tensor", lambda k=k: EN["tensor"].matmul(
                    PS[0:R, 3072:3136], wbsT[:, k * R:(k + 1) * R],
                    wcr[:, k * R:(k + 1) * R], start=(k == 0), stop=(k == 1)),
                    reads=["wbsT", "wcr"], writes=psk(3072, 3136), inc=(k == 1))
            op("vector", lambda: EN["vector"].tensor_copy(wmid[:, :], PS[0:R, 3072:3136]),
               reads=psk(3072, 3136), writes=["wmid"])

        def sa_layernorm(P, x0, ln, LNSL):
            S = sa[P]
            op("vector", lambda: EN["vector"].reduce_sum(S["sstat"][:], x0[:], axis=AX.X),
               reads=[f"x0t{P}"], writes=[f"sstat{P}"])
            op("scalar", lambda: EN["scalar"].activation(
                ln[:], x0[:], AF.Square, accum_out=S["vstat"][:]),
                reads=[f"x0t{P}"], writes=[f"vstat{P}", f"lnt{P}_scr"] + LNSL)
            op("vector", lambda: EN["vector"].tensor_scalar_mul(S["mstat"][:], S["sstat"][:], 1.0 / D),
               reads=[f"sstat{P}"], writes=[f"mstat{P}"])
            op("vector", lambda: EN["vector"].tensor_scalar_mul(S["vstat"][:], S["vstat"][:], 1.0 / D),
               reads=[f"vstat{P}"], writes=[f"vstat{P}"])
            op("vector", lambda: EN["vector"].scalar_tensor_tensor(
                S["rstd"][:], S["mstat"][:], 1.0, S["mstat"][:], ALU.mult, ALU.mult),
                reads=[f"mstat{P}"], writes=[f"rstd{P}"])
            op("vector", lambda: EN["vector"].tensor_sub(S["vstat"][:], S["vstat"][:], S["rstd"][:]),
               reads=[f"vstat{P}", f"rstd{P}"], writes=[f"vstat{P}"])
            op("scalar", lambda: EN["scalar"].activation(S["rstd"][:], S["vstat"][:], AF.Sqrt,
                                                         bias=epsb[:, 0:1]),
               reads=[f"vstat{P}", "epsb"], writes=[f"rstd{P}"])
            op("vector", lambda: EN["vector"].reciprocal(S["rstd"][:], S["rstd"][:]),
               reads=[f"rstd{P}"], writes=[f"rstd{P}"])
            op("vector", lambda: EN["vector"].tensor_scalar(
                ln[:], x0[:], S["mstat"][:, 0:1], S["rstd"][:, 0:1], ALU.subtract, ALU.mult),
                reads=[f"x0t{P}", f"mstat{P}", f"rstd{P}"], writes=[f"lnt{P}_scr"] + LNSL)

        def sa_transpose_ln(P, UB, ln, ttl, to_bf):
            for q in range(KT // 4):
                qb = UB + (q % 2) * 512
                kts = list(range(4 * q, 4 * q + 4))
                for j, kt in enumerate(kts):
                    op("tensor", lambda kt=kt, slot=qb + j * 128, ln=ln: EN["tensor"].transpose(
                        PS[:, slot:slot + 128], ln[:, kt * 128:(kt + 1) * 128], ident_f[:]),
                        reads=[f"lnt{P}_sl{kt}", "ident_f"], writes=psk(qb, qb + 512),
                        inc=(j == 3))
                op("vector", lambda q=q, qb=qb, ln=ln: EN["vector"].tensor_copy(
                    ln[:, q * 512:(q + 1) * 512], PS[:, qb:qb + 512]),
                    reads=psk(qb, qb + 512),
                    writes=[f"lnt{P}_sl{kt}" for kt in kts])
                if to_bf:
                    op("scalar", lambda q=q, ln=ln, ttl=ttl: EN["scalar"].copy(
                        ln1t[:].rearrange("p (k c) -> p k c", c=1024)[:, 4 * q:4 * q + 4, ttl * 128:(ttl + 1) * 128],
                        ln[:, q * 512:(q + 1) * 512].rearrange("p (k c) -> p k c", c=128)),
                        reads=[f"lnt{P}_sl{kt}" for kt in kts],
                        writes=[f"ln1t_{kt}" for kt in kts])

        def sa_seg0(g, tile):
            P = tile % 2
            op("sync", lambda tile=tile, P=P: EN["sync"].dma_start(
                x0t[P][:], u_h[tile * 128:(tile + 1) * 128, :]),
                writes=[f"x0t{P}"], inc="dx0")
            sa_layernorm(P, x0t[P], lnt[P], [f"lnt{P}_sl{k}" for k in range(KT)])

        def sa_seg1(g, tile):
            P = tile % 2
            UB = 3072 if P == 0 else 0
            S = sa[P]
            sa_transpose_ln(P, UB, lnt[P], tile % TT_G, False)
            for kt in range(KT):
                op("tensor", lambda kt=kt, P=P, UB=UB: EN["tensor"].matmul(
                    PS[0:R, UB + 512:UB + 640], wbr[:, kt * R:(kt + 1) * R],
                    lnt[P][:, kt * 128:(kt + 1) * 128],
                    start=(kt == 0), stop=(kt == KT - 1)),
                    reads=[f"lnt{P}_sl{kt}", "wbr"], writes=psk(UB + 512, UB + 640),
                    inc=(kt == KT - 1))
            op("vector", lambda S=S, UB=UB: EN["vector"].tensor_copy(
                S["s1b"][:, :], PS[0:R, UB + 512:UB + 640]),
                reads=psk(UB + 512, UB + 640), writes=[f"s1b{P}"])
            op("tensor", lambda S=S, UB=UB: EN["tensor"].matmul(
                PS[0:R, UB + 512:UB + 640], wmid[:, :], S["s1b"][:, :], start=True, stop=True),
                reads=["wmid", f"s1b{P}"], writes=psk(UB + 512, UB + 640), inc=True)
            op("vector", lambda S=S, UB=UB: EN["vector"].tensor_copy(
                S["s2b"][:, :], PS[0:R, UB + 512:UB + 640]),
                reads=psk(UB + 512, UB + 640), writes=[f"s2b{P}"])

        def sa_seg2(g, tile):
            P = tile % 2
            UB = 3072 if P == 0 else 0
            S = sa[P]
            for kt in range(KT):
                slot = UB + (kt % 2) * 512
                op("tensor", lambda kt=kt, slot=slot, S=S: EN["tensor"].matmul(
                    PS[:, slot:slot + 128], wcm[:, kt * 128:(kt + 1) * 128], S["s2b"][:, :],
                    start=True, stop=True),
                    reads=["wcm", f"s2b{P}"], writes=psk(slot, slot + 128), inc=True)
                op("vector", lambda kt=kt, slot=slot, P=P: EN["vector"].scalar_tensor_tensor(
                    lnt[P][:, kt * 128:(kt + 1) * 128], lnt[P][:, kt * 128:(kt + 1) * 128],
                    dvr[:, kt:kt + 1], PS[:, slot:slot + 128], ALU.mult, ALU.add),
                    reads=[f"lnt{P}_sl{kt}", "dvr"] + psk(slot, slot + 128),
                    writes=[f"lnt{P}_sl{kt}"])
            for half in range(2):
                for k8 in range(8):
                    kt = half * 8 + k8
                    slot = UB + k8 * 128
                    op("tensor", lambda kt=kt, slot=slot, P=P: EN["tensor"].transpose(
                        PS[:, slot:slot + 128], lnt[P][:, kt * 128:(kt + 1) * 128], ident_f[:]),
                        reads=[f"lnt{P}_sl{kt}", "ident_f"], writes=psk(slot, slot + 128),
                        inc=(k8 % 4 == 3))
                op("vector", lambda half=half, P=P, UB=UB: EN["vector"].tensor_add(
                    x0t[P][:, half * 1024:(half + 1) * 1024],
                    x0t[P][:, half * 1024:(half + 1) * 1024], PS[:, UB:UB + 1024]),
                    reads=[f"x0t{P}"] + psk(UB, UB + 1024), writes=[f"x0t{P}"])
            op("sync", lambda tile=tile, P=P: EN["sync"].dma_start(
                x1s_h[tile, :, :], x0t[P][:]),
                reads=[f"x0t{P}"], writes=[f"hbm_x1_{tile}"],
                inc="dx1a" if tile % 2 == 0 else "dx1b")
            if "x1" in taps:
                op("sync", lambda tile=tile, P=P: EN["sync"].dma_start(
                    tap_h["x1"][tile * 128:(tile + 1) * 128, :], x0t[P][:]),
                    reads=[f"x0t{P}"], writes=["ser_ta" if tile % 2 == 0 else "ser_tb"],
                    inc="dtapa" if tile % 2 == 0 else "dtapb")

        def sa_seg3(g, tile):
            P = tile % 2
            UB = 3072 if P == 0 else 0
            sa_layernorm(P, x0t[P], lnt[P], [f"lnt{P}_sl{k}" for k in range(KT)])
            sa_transpose_ln(P, UB, lnt[P], tile % TT_G, True)

        def sa_seg4(g, tile):
            P = tile % 2
            UB = 3072 if P == 0 else 0
            S = sa[P]
            ttl = tile % TT_G
            for kt in range(KT):
                op("tensor", lambda kt=kt, P=P, UB=UB: EN["tensor"].matmul(
                    PS[0:E, UB + 512:UB + 640], cwt[:, kt * E:(kt + 1) * E],
                    lnt[P][:, kt * 128:(kt + 1) * 128],
                    start=(kt == 0), stop=(kt == KT - 1)),
                    reads=[f"lnt{P}_sl{kt}", "cwt"], writes=psk(UB + 512, UB + 640),
                    inc=(kt == KT - 1))
            op("vector", lambda S=S, UB=UB: EN["vector"].tensor_copy(
                S["c8b"][:, :], PS[0:E, UB + 512:UB + 640]),
                reads=psk(UB + 512, UB + 640), writes=[f"c8b{P}"])
            op("tensor", lambda S=S, UB=UB: EN["tensor"].transpose(
                PS[:, UB + 512:UB + 520], S["c8b"][:, :], ident_f[0:E, 0:E]),
                reads=[f"c8b{P}", "ident_f"], writes=psk(UB + 512, UB + 520), inc=True)
            op("vector", lambda S=S, UB=UB: EN["vector"].tensor_copy(
                S["bidt"][:], PS[:, UB + 512:UB + 520]),
                reads=psk(UB + 512, UB + 520), writes=[f"bidt{P}"])
            op("vector", lambda S=S: EN["vector"].tensor_add(S["bidt"][:], S["bidt"][:], cbb[:, :]),
               reads=[f"bidt{P}", "cbb"], writes=[f"bidt{P}"])
            op("scalar", lambda S=S: EN["scalar"].activation(S["bidt"][:], S["bidt"][:], AF.Sigmoid),
               reads=[f"bidt{P}"], writes=[f"bidt{P}"])
            op("vector", lambda S=S: EN["vector"].tensor_mul(S["bidt"][:], S["bidt"][:], wlb[:, :]),
               reads=[f"bidt{P}", "wlb"], writes=[f"bidt{P}"])
            op("vector", lambda S=S: EN["vector"].max(S["t8"][:], S["bidt"][:]),
               reads=[f"bidt{P}"], writes=[f"t8{P}"])
            op("vector", lambda S=S: EN["vector"].tensor_single_scalar(
                S["pad1"][:], ones8[:], S["t8"][:, 0:1], ALU.mult),
                reads=["ones8", f"t8{P}"], writes=[f"pad1{P}"])
            op("vector", lambda S=S: EN["vector"].match_replace(
                S["rep"][:], S["pad1"][:, :], S["bidt"][:], -1e30),
                reads=[f"pad1{P}", f"bidt{P}"], writes=[f"rep{P}"])
            op("vector", lambda S=S: EN["vector"].tensor_single_scalar(
                S["eqA"][:], S["rep"][:], -1e29, ALU.is_lt),
                reads=[f"rep{P}"], writes=[f"eqA{P}"])
            op("vector", lambda S=S: EN["vector"].tensor_single_scalar(
                S["pad2"][:], ones8[:], S["t8"][:, 0:1], ALU.mult),
                reads=["ones8", f"t8{P}"], writes=[f"pad2{P}"])
            op("vector", lambda S=S: EN["vector"].tensor_copy(S["pad2"][:, 1:2], S["t8"][:, 1:2]),
               reads=[f"t8{P}"], writes=[f"pad2{P}"])
            op("vector", lambda S=S: EN["vector"].match_replace(
                S["rep"][:], S["pad2"][:, :], S["bidt"][:], -1e30),
                reads=[f"pad2{P}", f"bidt{P}"], writes=[f"rep{P}"])
            op("vector", lambda S=S: EN["vector"].tensor_single_scalar(
                S["eqB"][:], S["rep"][:], -1e29, ALU.is_lt),
                reads=[f"rep{P}"], writes=[f"eqB{P}"])
            op("vector", lambda S=S: EN["vector"].tensor_sub(S["d12"][:], S["t8"][:, 0:1], S["t8"][:, 1:2]),
               reads=[f"t8{P}"], writes=[f"d12{P}"])
            op("scalar", lambda S=S: EN["scalar"].activation(S["r1"][:, 0:1], S["d12"][:, 0:1], AF.Sigmoid),
               reads=[f"d12{P}"], writes=[f"r1{P}"])
            op("scalar", lambda S=S: EN["scalar"].activation(
                S["r2"][:, 0:1], S["r1"][:, 0:1], AF.Copy, scale=-1.0, bias=1.0),
                reads=[f"r1{P}"], writes=[f"r2{P}"])
            op("vector", lambda S=S: EN["vector"].scalar_tensor_tensor(
                S["wal"][:], S["eqA"][:], S["r1"][:, 0:1], S["eqA"][:], ALU.mult, ALU.bypass),
                reads=[f"eqA{P}", f"r1{P}"], writes=[f"wal{P}"])
            op("vector", lambda S=S: EN["vector"].tensor_sub(S["eqB"][:], S["eqB"][:], S["eqA"][:]),
               reads=[f"eqB{P}", f"eqA{P}"], writes=[f"eqB{P}"])
            op("vector", lambda S=S: EN["vector"].scalar_tensor_tensor(
                S["wal"][:], S["eqB"][:], S["r2"][:, 0:1], S["wal"][:], ALU.mult, ALU.add),
                reads=[f"eqB{P}", f"r2{P}", f"wal{P}"], writes=[f"wal{P}"])
            if "weall" in taps:
                op("sync", lambda tile=tile, S=S: EN["sync"].dma_start(
                    tap_h["weall"][tile * 128:(tile + 1) * 128, :], S["wal"][:]),
                    reads=[f"wal{P}"], writes=["ser_ta" if tile % 2 == 0 else "ser_tb"],
                    inc="dtapa" if tile % 2 == 0 else "dtapb")
            op("tensor", lambda S=S, UB=UB: EN["tensor"].transpose(
                PS[0:E, UB + 512:UB + 640], S["wal"][:], ident_f[:]),
                reads=[f"wal{P}", "ident_f"], writes=psk(UB + 512, UB + 640), inc=True)
            op("vector", lambda S=S, UB=UB: EN["vector"].tensor_copy(
                S["wet"][:, :], PS[0:E, UB + 512:UB + 640]),
                reads=psk(UB + 512, UB + 640), writes=[f"wet{P}"])
            for eh in range(2):
                for j in range(4):
                    e = eh * 4 + j
                    c0 = UB + eh * 512 + j * 128
                    op("tensor", lambda e=e, c0=c0, S=S: EN["tensor"].matmul(
                        PS[:, c0:c0 + 128], ohc[:, e * 128:(e + 1) * 128], S["wet"][:, :],
                        start=True, stop=True),
                        reads=["ohc", f"wet{P}"], writes=psk(c0, c0 + 128),
                        inc=(j == 3))
                op("vector", lambda eh=eh, ttl=ttl, UB=UB: EN["vector"].tensor_copy(
                    webc[:].rearrange("p (e c) -> p e c", c=1024)
                        [:, eh * 4:eh * 4 + 4, ttl * 128:(ttl + 1) * 128],
                    PS[:, UB + eh * 512: UB + eh * 512 + 512]
                        .rearrange("p (e c) -> p e c", c=128)),
                    reads=psk(UB + eh * 512, UB + eh * 512 + 512),
                    writes=[f"webc_{eh * 4 + j}" for j in range(4)])

        def emit_stage_a_group(g):
            segs = [sa_seg0, sa_seg1, sa_seg2, sa_seg3, sa_seg4]
            tiles = [g * TT_G + i for i in range(TT_G)]
            for r in range(2 * len(tiles) + len(segs)):
                for idx, t in enumerate(tiles):
                    j = r - 2 * idx
                    if 0 <= j < len(segs):
                        segs[j](g, t)

        def emit_xga(g):
            for e in range(E):
                buf, bk = wbuf[e % 3], f"wbuf{e % 3}"
                op("gpsimd", lambda e=e, buf=buf: EN["gpsimd"].dma_start(
                    buf[:, 0:KT * 128], xau_h[e]), writes=[bk], inc="dwb%d" % (e % 3))
                for kt in range(KT):
                    for fc in range(2):
                        c0 = 2048 + fc * 512
                        op("tensor", lambda kt=kt, fc=fc, buf=buf, c0=c0: EN["tensor"].matmul(
                            PS[:, c0:c0 + 512], buf[:, kt * 128:(kt + 1) * 128],
                            ln1t[:, kt * 1024 + fc * 512: kt * 1024 + (fc + 1) * 512],
                            start=(kt == 0), stop=(kt == KT - 1)),
                            reads=[bk, f"ln1t_{kt}"], writes=psk(c0, c0 + 512),
                            inc=(kt == KT - 1))
                sg_, su_ = e, 8 + e
                sgn, sun = (e + 1) % 8, 8 + (e + 1) % 8
                op("vector", lambda sg_=sg_: EN["vector"].tensor_single_scalar(
                    xga[0:64, sg_ * 1024:(sg_ + 1) * 1024], PS[0:64, 2048:3072],
                    SCALING, ALU.mult),
                    reads=psk(2048, 3072), writes=[f"xga_{sg_}a"])
                op("vector", lambda su_=su_: EN["vector"].tensor_single_scalar(
                    xga[0:64, su_ * 1024:(su_ + 1) * 1024], PS[64:128, 2048:3072],
                    SCALING, ALU.mult),
                    reads=psk(2048, 3072), writes=[f"xga_{su_}a"])
                if e < E - 1:
                    op("vector", lambda sgn=sgn: EN["vector"].tensor_single_scalar(
                        xga[64:128, sgn * 1024:(sgn + 1) * 1024], PS[0:64, 2048:3072],
                        -SCALING, ALU.mult),
                        reads=psk(2048, 3072), writes=[f"xga_{sgn}b"])
                    op("vector", lambda sun=sun: EN["vector"].tensor_single_scalar(
                        xga[64:128, sun * 1024:(sun + 1) * 1024], PS[64:128, 2048:3072],
                        -SCALING, ALU.mult),
                        reads=psk(2048, 3072), writes=[f"xga_{sun}b"])
            op("vector", lambda: EN["vector"].memset(xga[64:128, 0:1024], 0.0),
               writes=["xga_0b"])
            op("vector", lambda: EN["vector"].memset(xga[64:128, 8 * 1024:9 * 1024], 0.0),
               writes=["xga_8b"])

        def p1_regions(it):
            if it % 2 == 0:
                return 0, 1024
            return 2048, 3072

        def emit_p1_dma(g, it):
            buf, bk = wbuf[it % 3], f"wbuf{it % 3}"
            op("gpsimd", lambda it=it, buf=buf: EN["gpsimd"].dma_start(
                buf[:], wslab_h[it]), writes=[bk], inc="dwb%d" % (it % 3))

        def emit_p1_base_chunk(g, it, which, kt):
            buf, bk = wbuf[it % 3], f"wbuf{it % 3}"
            G0, U0 = p1_regions(it)
            base, col = (WC_BG, G0) if which == "g" else (WC_BU, U0)
            for fc in range(2):
                op("tensor", lambda kt=kt, fc=fc, buf=buf, base=base, col=col:
                    EN["tensor"].matmul(
                        PS[:, col + fc * 512: col + (fc + 1) * 512],
                        buf[:, base + kt * 128: base + (kt + 1) * 128],
                        ln1t[:, kt * 1024 + fc * 512: kt * 1024 + (fc + 1) * 512],
                        start=(kt == 0), stop=False, skip_group_check=True),
                    reads=[bk, f"ln1t_{kt}"], writes=[f"B{(col + fc * 512) // 512}"],
                    inc=False)

        def emit_p1_expert(g, it, e):
            buf, bk = wbuf[it % 3], f"wbuf{it % 3}"
            G0, U0 = p1_regions(it)
            if e == 0:
                op("scalar", lambda buf=buf, it=it: EN["scalar"].copy(
                    dabuf[it % 2][:], buf[:, WC_DA:WC_DA + E * R]),
                    reads=[bk], writes=[f"dabuf{it % 2}"])
            sg_, su_ = e, 8 + e
            for fc in range(2):
                op("tensor", lambda e=e, fc=fc, buf=buf, sg_=sg_, G0=G0:
                    EN["tensor"].matmul(
                        PS[:, G0 + fc * 512: G0 + (fc + 1) * 512],
                        buf[:, WC_GB + e * 128: WC_GB + (e + 1) * 128],
                        xga[:, sg_ * 1024 + fc * 512: sg_ * 1024 + (fc + 1) * 512],
                        start=False, stop=(e == E - 1), skip_group_check=True),
                    reads=[bk, f"xga_{sg_}a", f"xga_{sg_}b"],
                    writes=[f"B{(G0 + fc * 512) // 512}"], inc=True)
                op("tensor", lambda e=e, fc=fc, buf=buf, su_=su_, U0=U0:
                    EN["tensor"].matmul(
                        PS[:, U0 + fc * 512: U0 + (fc + 1) * 512],
                        buf[:, WC_UB + e * 128: WC_UB + (e + 1) * 128],
                        xga[:, su_ * 1024 + fc * 512: su_ * 1024 + (fc + 1) * 512],
                        start=False, stop=(e == E - 1), skip_group_check=True),
                    reads=[bk, f"xga_{su_}a", f"xga_{su_}b"],
                    writes=[f"B{(U0 + fc * 512) // 512}"], inc=True)
            sg2, ub2 = sgb[e % 2], ubb[e % 2]
            sgk, ubk = f"sgb{e % 2}", f"ubb{e % 2}"
            op("scalar", lambda sg2=sg2, G0=G0: EN["scalar"].activation(
                sg2[:], PS[:, G0:G0 + 1024], AF.Silu),
                reads=psk(G0, G0 + 1024), writes=[sgk])
            op("scalar", lambda ub2=ub2, U0=U0: EN["scalar"].copy(
                ub2[:], PS[:, U0:U0 + 1024]),
                reads=psk(U0, U0 + 1024), writes=[ubk])
            op("vector", lambda e=e, sg2=sg2, ub2=ub2: EN["vector"].tensor_mul(
                h8[:, e * 1024:(e + 1) * 1024], sg2[:], ub2[:]),
                reads=[sgk, ubk], writes=[f"h8_{e}"])
            hwv, hwk = hmixw[it % 2], f"hmixw{it % 2}"
            if e == 0:
                op("vector", lambda hwv=hwv: EN["vector"].tensor_mul(
                    hwv[:], h8[:, 0:1024], webc[:, 0:1024]),
                    reads=["h8_0", "webc_0"], writes=[hwk])
            else:
                op("vector", lambda e=e, sg2=sg2: EN["vector"].tensor_mul(
                    sg2[:], h8[:, e * 1024:(e + 1) * 1024],
                    webc[:, e * 1024:(e + 1) * 1024]),
                    reads=[f"h8_{e}", f"webc_{e}"], writes=[sgk])
                op("vector", lambda hwv=hwv, sg2=sg2: EN["vector"].tensor_add(
                    hwv[:], hwv[:], sg2[:]), reads=[hwk, sgk], writes=[hwk])

        def emit_p1_hmix_store(g, it):
            hwv, hwk = hmixw[it % 2], f"hmixw{it % 2}"
            op("sync", lambda it=it, g=g, hwv=hwv: EN["sync"].dma_start(
                hx_h[it, :, g * TG:(g + 1) * TG], hwv[:]),
                reads=[hwk], writes=[f"hbm_hx_{it}_{g}"], inc="dhx%d" % (it % 2))
            if "hmix" in taps:
                op("sync", lambda it=it, g=g, hwv=hwv: EN["sync"].dma_start(
                    tap_h["hmix"][it, :, g * TG:(g + 1) * TG], hwv[:]),
                    reads=[hwk], writes=["ser_ta" if it % 2 == 0 else "ser_tb"],
                    inc="dtapa" if it % 2 == 0 else "dtapb")

        def emit_p1_dA_pair(g, it, pr):
            dab, dak = dabuf[it % 2], f"dabuf{it % 2}"
            G0r, U0r = p1_regions(it)
            G0 = G0r if pr < 2 else U0r
            for hilo in range(2):
                e = 2 * pr + hilo
                for fc in range(2):
                    op("tensor", lambda e=e, fc=fc, hilo=hilo, dab=dab, G0=G0:
                        EN["tensor"].matmul(
                            PS[hilo * 64:(hilo + 1) * 64, G0 + fc * 512: G0 + (fc + 1) * 512],
                            dab[:, e * 64:(e + 1) * 64],
                            h8[:, e * 1024 + fc * 512: e * 1024 + (fc + 1) * 512],
                            start=True, stop=True),
                        reads=[dak, f"h8_{e}"],
                        writes=[f"B{(G0 + fc * 512) // 512}"], inc=True)
            for fc in range(2):
                c0 = G0 + fc * 512
                hs0 = pr * 1024 + fc * 512
                if it == 0:
                    op("vector", lambda c0=c0, hs0=hs0: EN["vector"].tensor_single_scalar(
                        hab[:, hs0:hs0 + 512], PS[:, c0:c0 + 512], SCALING, ALU.mult),
                        reads=psk(c0, c0 + 512), writes=[f"hab_{pr}_{fc}"])
                else:
                    op("vector", lambda c0=c0, hs0=hs0: EN["vector"].scalar_tensor_tensor(
                        hab[:, hs0:hs0 + 512], PS[:, c0:c0 + 512], SCALING,
                        hab[:, hs0:hs0 + 512], ALU.mult, ALU.add),
                        reads=psk(c0, c0 + 512) + [f"hab_{pr}_{fc}"],
                        writes=[f"hab_{pr}_{fc}"])

        def emit_phase1_group(g):
            emit_p1_dma(g, 0)
            emit_p1_dma(g, 1)
            for kt in range(KT):
                emit_p1_base_chunk(g, 0, "u", kt)
            for kt in range(KT):
                emit_p1_base_chunk(g, 0, "g", kt)
            for it in range(NIT):
                if it + 2 < NIT:
                    emit_p1_dma(g, it + 2)
                for e in range(E):
                    if it > 0 and e < 4:
                        emit_p1_dA_pair(g, it - 1, e)
                    if it + 1 < NIT:
                        if e in (2, 3):
                            for kt in range(8 * (e - 2), 8 * (e - 2) + 8):
                                emit_p1_base_chunk(g, it + 1, "g", kt)
                        elif e >= 4:
                            for kt in range(4 * (e - 4), 4 * (e - 4) + 4):
                                emit_p1_base_chunk(g, it + 1, "u", kt)
                    emit_p1_expert(g, it, e)
                emit_p1_hmix_store(g, it)
            for pr in range(4):
                emit_p1_dA_pair(g, NIT - 1, pr)

        def emit_haw(g):
            for pr in range(4):
                for hilo in range(2):
                    e = 2 * pr + hilo
                    op("vector", lambda pr=pr, hilo=hilo, e=e: EN["vector"].tensor_mul(
                        h8[hilo * 64:(hilo + 1) * 64, pr * 1024:(pr + 1) * 1024],
                        hab[hilo * 64:(hilo + 1) * 64, pr * 1024:(pr + 1) * 1024],
                        webc[hilo * 64:(hilo + 1) * 64, e * 1024:(e + 1) * 1024]),
                        reads=[f"hab_{pr}_0", f"hab_{pr}_1", f"webc_{e}"],
                        writes=[f"hawstg_{pr}", f"h8_{pr}"])
            for pr in range(4):
                op("sync", lambda pr=pr, g=g: EN["sync"].dma_start(
                    haw_h[:, pr, g, :], h8[:, pr * 1024:(pr + 1) * 1024]),
                    reads=[f"hawstg_{pr}", f"h8_{pr}"],
                    writes=[f"hbm_haw_{g}"],
                    inc="dhw0" if g == 0 else "dhw1")

        def emit_phase2():
            GA = 7
            NG = NIT // GA
            op("scalar", lambda: EN["scalar"].dma_start(
                hawb[:], haw_h.rearrange("p a b c -> p (a b c)")),
                reads=["hbm_haw_0", "hbm_haw_1"], writes=["hawball"], inc="dhg0")
            for dq in range(4):
                for gi in range(NG):
                    buf, bk = wbuf[gi % 2], f"wbuf{gi % 2}"
                    hg, hk = hmg[gi % 2], f"hmg{gi % 2}"
                    op("gpsimd", lambda dq=dq, gi=gi, buf=buf: EN["gpsimd"].dma_start(
                        buf[:, 0:GA * 512].rearrange("p (g c) -> p g c", g=GA),
                        bd_h[dq, gi * GA:(gi + 1) * GA].rearrange("g p c -> p g c")),
                        writes=[bk + "bd"], inc="dwb%d" % (gi % 2))
                    for j in range(GA):
                        it = gi * GA + j
                        rd = [f"hbm_hx_{it}_{gg}" for gg in range(NGRP)]
                        op("scalar", lambda j=j, it=it, hg=hg: EN["scalar"].dma_start(
                            hg[:, j * 2048:(j + 1) * 2048], hx_h[it, :, :]),
                            reads=rd, writes=[hk + f"x{j}"], inc="dhg%d" % (gi % 2))
                    tk.mark_done([hk + f"x{j}" for j in range(GA)], "dhg%d" % (gi % 2))
                    for dtl in range(4):
                        pc = 0 if dtl % 2 == 0 else 2048
                        for j in range(GA):
                            for fc in range(4):
                                op("tensor", lambda dtl=dtl, j=j, fc=fc, buf=buf, hg=hg, pc=pc:
                                    EN["tensor"].matmul(
                                        PS[:, pc + fc * 512: pc + (fc + 1) * 512],
                                        buf[:, j * 512 + dtl * 128: j * 512 + (dtl + 1) * 128],
                                        hg[:, j * 2048 + fc * 512: j * 2048 + (fc + 1) * 512],
                                        start=(j == 0), stop=(j == GA - 1)),
                                    reads=[bk + "bd", hk + f"x{j}"],
                                    writes=[f"B{(pc + fc * 512) // 512}"],
                                    inc=(j == GA - 1))
                        if gi == 0:
                            op("vector", lambda dtl=dtl, pc=pc: EN["vector"].tensor_copy(
                                outacc[:, dtl * 2048:(dtl + 1) * 2048], PS[:, pc:pc + 2048]),
                                reads=psk(pc, pc + 2048), writes=[f"oa_{dtl}"])
                        else:
                            op("vector", lambda dtl=dtl, pc=pc: EN["vector"].tensor_add(
                                outacc[:, dtl * 2048:(dtl + 1) * 2048],
                                outacc[:, dtl * 2048:(dtl + 1) * 2048], PS[:, pc:pc + 2048]),
                                reads=psk(pc, pc + 2048) + [f"oa_{dtl}"], writes=[f"oa_{dtl}"])
                op("gpsimd", lambda dq=dq: EN["gpsimd"].dma_start(
                    wbuf[0][:, 0:4 * 512].rearrange("p (q c) -> p q c", q=4),
                    dbcat_h[:, :, dq * 512:(dq + 1) * 512].rearrange("q p c -> p q c")),
                    writes=["wbuf0bd"], inc="dwb0")
                for dtl in range(4):
                    pc = 0 if dtl % 2 == 0 else 2048
                    for pr in range(4):
                        for fc in range(4):
                            hkey = ["hawball"]
                            op("tensor", lambda dtl=dtl, pr=pr, fc=fc, pc=pc:
                                EN["tensor"].matmul(
                                    PS[:, pc + fc * 512: pc + (fc + 1) * 512],
                                    wbuf[0][:, pr * 512 + dtl * 128: pr * 512 + (dtl + 1) * 128],
                                    hawb[:, pr * TC + fc * 512: pr * TC + (fc + 1) * 512],
                                    start=(pr == 0), stop=(pr == 3)),
                                reads=["wbuf0bd"] + hkey,
                                writes=[f"B{(pc + fc * 512) // 512}"],
                                inc=(pr == 3))
                    op("vector", lambda dtl=dtl, pc=pc: EN["vector"].tensor_add(
                        outacc[:, dtl * 2048:(dtl + 1) * 2048],
                        outacc[:, dtl * 2048:(dtl + 1) * 2048], PS[:, pc:pc + 2048]),
                        reads=psk(pc, pc + 2048) + [f"oa_{dtl}"], writes=[f"oa_{dtl}"])
                def x1load(tt, dq=dq):
                    xb, xk = x1buf[tt % 4], f"x1buf{tt % 4}"
                    op("sync", lambda tt=tt, dq=dq, xb=xb: EN["sync"].dma_start(
                        xb[:], x1s_h[tt, :, dq * 512:(dq + 1) * 512]),
                        reads=[f"hbm_x1_{tt}"], writes=[xk], inc="dxb%d" % (tt % 4))
                x1load(0)
                x1load(1)
                for tt in range(NTT):
                    xb, xk = x1buf[tt % 4], f"x1buf{tt % 4}"
                    orow, ork = outrow[tt % 4], f"outrow{tt % 4}"
                    if tt + 2 < NTT:
                        x1load(tt + 2)
                    for dtl in range(4):
                        slot = 3072 + dtl * 128
                        op("tensor", lambda dtl=dtl, tt=tt, slot=slot: EN["tensor"].transpose(
                            PS[:, slot:slot + 128],
                            outacc[:, dtl * 2048 + tt * 128: dtl * 2048 + (tt + 1) * 128],
                            ident_f[:]),
                            reads=[f"oa_{dtl}", "ident_f"], writes=psk(slot, slot + 128),
                            inc=(dtl == 3))
                    op("vector", lambda xb=xb, orow=orow: EN["vector"].tensor_add(
                        orow[:], PS[:, 3072:3584], xb[:]),
                        reads=psk(3072, 3584) + [xk], writes=[ork])
                    op("sync", lambda tt=tt, dq=dq, orow=orow: EN["sync"].dma_start(
                        y_h[tt * 128:(tt + 1) * 128, dq * 512:(dq + 1) * 512], orow[:]),
                        reads=[ork], writes=["hbm_y"], inc="dya" if tt % 2 == 0 else "dyb")

        def build_all():
            stage = [0]

            def go():
                stage[0] += 1
                return stop_after is None or stage[0] <= stop_after
            emit_init()
            done = False
            for g in range(NGRP):
                if g > 0:
                    tk.barrier()
                if go():
                    emit_stage_a_group(g)
                else:
                    done = True
                if not done and go():
                    emit_xga(g)
                else:
                    done = True
                tk.barrier()
                if not done and go():
                    emit_phase1_group(g)
                else:
                    done = True
                if not done and go():
                    emit_haw(g)
                else:
                    done = True
            tk.barrier()
            if not done and go():
                emit_phase2()
            tk.barrier()

        for engname in ENGS:
            def body(engobj, engname=engname):
                EN.clear()
                EN[engname] = engobj
                tk.reset(engname, engobj)
                build_all()
            getattr(blk, engname)(body)

    return nc, sorted(taps)



def _prep_inputs(L0_attn_u, L0_attn_A_factors, L0_attn_W_b_rank, L0_attn_W_b_state,
                 L0_attn_W_c_rank, L0_attn_W_c_model, L0_attn_D_vec,
                 L0_ffn_expert_wealth, L0_ffn_conf_W, L0_ffn_conf_b,
                 L0_ffn_base_gate, L0_ffn_base_up, L0_ffn_base_down,
                 L0_ffn_gate_A, L0_ffn_gate_B, L0_ffn_up_A, L0_ffn_up_B,
                 L0_ffn_down_A, L0_ffn_down_B):
    f = lambda a: np.ascontiguousarray(np.asarray(a), dtype=np.float32)
    u = f(L0_attn_u).reshape(NCORES * TC, D)
    bg = f(L0_ffn_base_gate).reshape(KT, 128, NIT, 128).transpose(2, 1, 0, 3).reshape(NIT, 128, KT * 128)
    bu = f(L0_ffn_base_up).reshape(KT, 128, NIT, 128).transpose(2, 1, 0, 3).reshape(NIT, 128, KT * 128)

    def gbcat(W):
        Wt = f(W).reshape(E, R, NIT, 128)
        cur = Wt.transpose(2, 1, 0, 3)
        prv = Wt[(np.arange(E) - 1) % E].transpose(2, 1, 0, 3)
        return np.concatenate([cur, prv], axis=1).reshape(NIT, 128, E * 128)

    dA = f(L0_ffn_down_A).reshape(E, NIT, 128, R).transpose(1, 2, 0, 3).reshape(NIT, 128, E * R)
    wslab = np.ascontiguousarray(
        np.concatenate([bg, bu, gbcat(L0_ffn_gate_B), gbcat(L0_ffn_up_B), dA], axis=2))
    gA = f(L0_ffn_gate_A).reshape(E, KT, 128, R)
    uA = f(L0_ffn_up_A).reshape(E, KT, 128, R)
    xau = np.concatenate([gA, uA], axis=3)
    xau = np.ascontiguousarray(xau.transpose(0, 2, 1, 3).reshape(E, 128, KT * 128))
    bd = np.ascontiguousarray(
        f(L0_ffn_base_down).reshape(NIT, 128, 4, 512).transpose(2, 0, 1, 3))
    dbcat = np.ascontiguousarray(f(L0_ffn_down_B).reshape(4, 2 * R, D))
    return {
        "u": u,
        "wslab": wslab,
        "xau": xau,
        "bdslab": bd,
        "dbcat": dbcat,
        "wbr": f(L0_attn_W_b_rank),
        "wbs": f(L0_attn_W_b_state),
        "wcr": f(L0_attn_W_c_rank),
        "wcm": f(L0_attn_W_c_model),
        "afac": f(L0_attn_A_factors).reshape(1, 32),
        "dvecr": np.ascontiguousarray(f(L0_attn_D_vec).reshape(KT, 128).T),
        "confwt": np.ascontiguousarray(f(L0_ffn_conf_W).T),
        "confb": f(L0_ffn_conf_b).reshape(1, E),
        "wealth": f(L0_ffn_expert_wealth).reshape(1, E),
    }


def kernel(**inputs):
    ins = _prep_inputs(**inputs)
    nc, _ = build_core_program()
    u = ins.pop("u")
    in_maps = []
    for c in range(NCORES):
        m = dict(ins)
        m["u"] = np.ascontiguousarray(u[c * TC:(c + 1) * TC])
        in_maps.append(m)
    res = run_bass_kernel_spmd(nc, in_maps, core_ids=list(range(NCORES)))
    out = np.concatenate([r["y"] for r in res.results], axis=0)
    return out.reshape(2, 8192, 2048)
